# version 08
# baseline (speedup 1.0000x reference)
import numpy as np
import ml_dtypes
from contextlib import ExitStack

import concourse.bass as bass
import concourse.mybir as mybir
from concourse.bass_utils import run_bass_kernel_spmd

F32 = mybir.dt.float32
BF16 = mybir.dt.bfloat16
AF = mybir.ActivationFunctionType
ALU = mybir.AluOpType
AXX = mybir.AxisListType.X

D = 4096
DFF = 11008
NROW = 32
NQT = 8
NFT = 86
NDB = 8
NH = 4
NCORE = 8

ENGINES = ("sync", "scalar", "vector", "tensor", "gpsimd")


class Sem:
    def __init__(self, handle, name):
        self.h = handle
        self.name = name
        self.n = 0


class Prog:

    def __init__(self, chain_sems):
        self.records = []
        self.chain = chain_sems
        self.last_writer = {}
        self.readers = {}
        self.rec_engine = []
        self.rec_dma = []
        self.rec_inc = []
        self.eng_incs = {e: [] for e in ENGINES}

    def _cover(self, idx):
        e = self.rec_engine[idx]
        if self.rec_dma[idx]:
            inc = self.rec_inc[idx]
            assert inc is not None, f"DMA record {idx} has no completion inc"
            return (inc[0], inc[2])
        for (ridx, sem, val) in self.eng_incs[e]:
            if ridx >= idx:
                return (sem, val)
        raise RuntimeError(f"no cover for record {idx} on engine {e}")

    def emit(self, eng, fn, reads=(), writes=(), inc=None, extra_waits=(),
             dma=False, auto_inc=True, serialize=False):
        idx = len(self.records)
        waits = {}
        if serialize and inc is not None:
            extra_waits = list(extra_waits) + [(inc[0], inc[0].n)]

        def add_wait(sem, val):
            if val > 0 and waits.get(id(sem), (None, -1))[1] < val:
                waits[id(sem)] = (sem, val)

        for sem, val in extra_waits:
            add_wait(sem, val)

        def dep(pidx):
            if self.rec_engine[pidx] == "tensor" and eng == "tensor" \
                    and not self.rec_dma[pidx] and not dma:
                return
            sem, val = self._cover(pidx)
            add_wait(sem, val)

        for key in reads:
            w = self.last_writer.get(key)
            if w is not None:
                dep(w)
        for key in writes:
            for r in self.readers.get(key, ()):
                dep(r)
            w = self.last_writer.get(key)
            if w is not None:
                dep(w)

        for key in reads:
            self.readers.setdefault(key, []).append(idx)
        for key in writes:
            self.last_writer[key] = idx
            self.readers[key] = []

        if inc is None and auto_inc and not dma and eng in ("vector", "scalar", "gpsimd"):
            inc = (self.chain[eng], 1)
        rec_inc = None
        if inc is not None:
            sem, amt = inc
            sem.n += amt
            rec_inc = (sem, amt, sem.n)
            self.eng_incs[eng].append((idx, sem, sem.n))

        self.records.append((eng, fn, sorted(waits.values(), key=lambda t: id(t[0])), rec_inc))
        self.rec_engine.append(eng)
        self.rec_dma.append(dma)
        self.rec_inc.append(rec_inc)
        return idx

    def build(self, block):
        last_wait = {e: {} for e in ENGINES}
        stats = {e: [0, 0, 0] for e in ENGINES}

        def run_engine(name, eng):
            lw = last_wait[name]
            for (rec_eng, fn, waits, inc) in self.records:
                if rec_eng != name:
                    continue
                stats[name][0] += 1
                for sem, val in waits:
                    if lw.get(id(sem), -1) >= val:
                        stats[name][2] += 1
                        continue
                    eng.wait_ge(sem.h, val)
                    lw[id(sem)] = val
                    stats[name][1] += 1
                ins = fn(eng)
                if inc is not None:
                    ins.then_inc(inc[0].h, inc[1])

        @block.sync
        def _(e):
            run_engine("sync", e)

        @block.scalar
        def _(e):
            run_engine("scalar", e)

        @block.vector
        def _(e):
            run_engine("vector", e)

        @block.tensor
        def _(e):
            run_engine("tensor", e)

        @block.gpsimd
        def _(e):
            run_engine("gpsimd", e)

        return stats


def build_program():
    nc = bass.Bass("TRN2", target_bir_lowering=False, debug=False)

    x_d = nc.dram_tensor("x", [NROW * 128, D], F32, kind="ExternalInput").ap()
    wu_d = nc.dram_tensor("wu", [NFT, 128, D], F32, kind="ExternalInput").ap()
    wg_d = nc.dram_tensor("wg", [NFT, 128, D], F32, kind="ExternalInput").ap()
    wd_d = nc.dram_tensor("wd", [NDB, NFT, 128, 512], F32, kind="ExternalInput").ap()
    bq_d = nc.dram_tensor("bq", [4, 1024], F32, kind="ExternalInput").ap()
    bk_d = nc.dram_tensor("bk", [4, 1024], F32, kind="ExternalInput").ap()
    bv_d = nc.dram_tensor("bv", [4, 1024], F32, kind="ExternalInput").ap()
    msk_d = nc.dram_tensor("msk", [8, 128, 256], BF16, kind="ExternalInput").ap()
    idn_d = nc.dram_tensor("idn", [128, 128], BF16, kind="ExternalInput").ap()
    i32_d = nc.dram_tensor("i32", [32, 32], F32, kind="ExternalInput").ap()
    out_d = nc.dram_tensor("out", [1024, D], F32, kind="ExternalOutput").ap()

    xnt_d = nc.dram_tensor("xnt_s", [D, NROW * 128], BF16, kind="Internal").ap()
    xna_d = nc.dram_tensor("xna_s", [NROW * 128, D], BF16, kind="Internal").ap()
    h_d = nc.dram_tensor("h_s", [NFT, 128, 1024], BF16, kind="Internal").ap()
    s_d = nc.dram_tensor("s_s", [1024, D], F32, kind="Internal").ap()
    qkv_d = nc.dram_tensor("qkv_s", [4, 1024], F32, kind="Internal").ap()
    vvf_d = nc.dram_tensor("vvf_s", [4, 1024], F32, kind="Internal").ap()

    st = ExitStack()

    def mksem(name):
        return Sem(st.enter_context(nc.semaphore(name)), name)

    with st:
        VC, SC, GC = mksem("VC"), mksem("SC"), mksem("GC")
        TC = mksem("TC")
        INID = mksem("INID")
        XD = [mksem(f"XD{i}") for i in range(3)]
        XSS = [[mksem(f"XS{i}_{q}") for q in range(4)] for i in range(2)]
        XA = [mksem(f"XA{i}") for i in range(3)]
        XO = [mksem(f"XO{i}") for i in range(4)]
        WGS = [mksem(f"WG{i}") for i in range(2)]
        WUS = [mksem(f"WU{i}") for i in range(2)]
        GT = mksem("GT")
        UPS = mksem("UPS")
        HDS = [mksem(f"HD{i}") for i in range(2)]
        VLS = [mksem(f"VL{i}") for i in range(4)]
        QSD = mksem("QSD")
        KVS = [mksem(f"KV{i}") for i in range(4)]
        KVF = mksem("KVF")
        TPS = mksem("TPS")
        PVD = mksem("PVD")
        SEV = mksem("SEV")
        SDS = [mksem(f"SD{i}") for i in range(2)]
        HLD = mksem("HLD")
        HLD2 = mksem("HLD2")
        WDS = [mksem(f"WD{i}") for i in range(3)]
        WDF = mksem("WDF")
        YDT = mksem("YDT")
        ODS = [mksem(f"OD{i}") for i in range(3)]
        STS = [mksem(f"ST{i}") for i in range(3)]
        XTS = [mksem(f"XT{i}") for i in range(3)]

        chain = {"vector": VC, "scalar": SC, "gpsimd": GC}
        P = Prog(chain)

        c_cst = st.enter_context(nc.sbuf_tensor("c_cst", [128, 4], F32))
        c_idn = st.enter_context(nc.sbuf_tensor("c_idn", [128, 128], BF16))

        att_live = ExitStack()
        qk_col = att_live.enter_context(nc.sbuf_tensor("qk_col", [128, 32], F32))
        v_bc = att_live.enter_context(nc.sbuf_tensor("v_bc", [128, D], F32))
        c_msk = att_live.enter_context(nc.sbuf_tensor("c_msk", [128, 8 * 256], BF16))

        with ExitStack() as ph:
            s_bq = ph.enter_context(nc.sbuf_tensor("s_bq", [4, 1024], F32))
            s_bk = ph.enter_context(nc.sbuf_tensor("s_bk", [4, 1024], F32))
            s_bv = ph.enter_context(nc.sbuf_tensor("s_bv", [4, 1024], F32))
            s_al = ph.enter_context(nc.sbuf_tensor("s_al", [4, 4], F32))
            s_tmp = ph.enter_context(nc.sbuf_tensor("s_tmp", [4, 1024], F32))
            s_i32 = ph.enter_context(nc.sbuf_tensor("s_i32", [32, 32], F32))
            s_q32 = ph.enter_context(nc.sbuf_tensor("s_q32", [32, 128], F32))
            s_vf = ph.enter_context(nc.sbuf_tensor("s_vf", [1, D], F32))
            s_one = ph.enter_context(nc.sbuf_tensor("s_one", [1, 128], F32))
            p_qk = ph.enter_context(nc.psum_tensor("p_qk", [128, 32], F32))
            p_vb = ph.enter_context(nc.psum_tensor("p_vb", [128, 512], F32))

            bufs = {"s_bq": s_bq, "s_bk": s_bk, "s_bv": s_bv}
            for dst, src, key in [(c_idn, idn_d, "c_idn"), (s_i32, i32_d, "s_i32"),
                                  (s_bq, bq_d, "s_bq"), (s_bk, bk_d, "s_bk"),
                                  (s_bv, bv_d, "s_bv")]:
                P.emit("sync", lambda e, dst=dst, src=src: e.dma_start(
                    out=dst[:], in_=src), writes=[(key,)], inc=(INID, 16), dma=True, serialize=True)
            P.emit("sync", lambda e: e.dma_start(
                out=c_msk[:].rearrange("p (r t) -> p r t", r=8),
                in_=msk_d.rearrange("r p t -> p r t")),
                writes=[("c_msk",)], inc=(INID, 16), dma=True, serialize=True)
            P.emit("vector", lambda e: e.memset(c_cst[:, 0:1], 1e-5), writes=[("c_cst",)])
            P.emit("vector", lambda e: e.memset(c_cst[:, 1:2], 0.0), writes=[("c_cst",)])
            P.emit("vector", lambda e: e.memset(s_one[:, :], 1.0), writes=[("s_one",)])
            for col, key in ((0, "s_bq"), (1, "s_bk"), (2, "s_bv")):
                P.emit("vector", lambda e, s_w=bufs[key], col=col: e.reduce_sum(
                    s_al[:, col:col + 1], s_w[:], AXX, apply_absolute_value=True),
                    reads=[(key,)], writes=[("s_al",)])
            P.emit("scalar", lambda e: e.mul(s_al[:, 0:3], s_al[:, 0:3], 1.0 / 1024),
                   reads=[("s_al",)], writes=[("s_al",)])
            for key in ("s_bq", "s_bk", "s_bv"):
                P.emit("vector", lambda e, s_w=bufs[key]: e.tensor_scalar(
                    s_w[:], s_w[:], c_cst[0:4, 1:2], None, ALU.is_ge),
                    reads=[(key,), ("c_cst",)], writes=[(key,)])
                P.emit("vector", lambda e, s_w=bufs[key]: e.tensor_scalar(
                    s_w[:], s_w[:], 2.0, -1.0, ALU.mult, ALU.add),
                    reads=[(key,)], writes=[(key,)])
            P.emit("vector", lambda e: e.tensor_mul(s_tmp[:], s_bq[:], s_bk[:]),
                   reads=[("s_bq",), ("s_bk",)], writes=[("s_tmp",)])
            P.emit("vector", lambda e: e.tensor_scalar(
                s_tmp[:], s_tmp[:], s_al[:, 0:1], s_al[:, 1:2], ALU.mult, ALU.mult),
                reads=[("s_tmp",), ("s_al",)], writes=[("s_tmp",)])
            P.emit("vector", lambda e: e.tensor_scalar(
                s_tmp[:], s_tmp[:], 0.125, None, ALU.mult),
                reads=[("s_tmp",)], writes=[("s_tmp",)])
            P.emit("vector", lambda e: e.tensor_scalar(
                s_bv[:], s_bv[:], s_al[:, 2:3], None, ALU.mult),
                reads=[("s_bv",), ("s_al",)], writes=[("s_bv",)])
            P.emit("sync", lambda e: e.dma_start(out=qkv_d, in_=s_tmp[:]),
                   reads=[("s_tmp",)], writes=[("qkv_d",)], inc=(INID, 16), dma=True, serialize=True)
            P.emit("sync", lambda e: e.dma_start(out=vvf_d, in_=s_bv[:]),
                   reads=[("s_bv",)], writes=[("vvf_d",)], inc=(INID, 16), dma=True, serialize=True)
            P.emit("sync", lambda e: e.dma_start(
                out=s_q32[:],
                in_=qkv_d.rearrange("a b -> (a b)").rearrange("(dt p) -> dt p", p=128)),
                reads=[("qkv_d",)], writes=[("s_q32",)], inc=(INID, 16), dma=True, serialize=True)
            P.emit("tensor", lambda e: e.matmul(
                p_qk[:], s_q32[:], s_i32[:], start=True, stop=True),
                reads=[("s_q32",), ("s_i32",)], writes=[("p_qk",)], inc=(TC, 1))
            P.emit("vector", lambda e: e.tensor_copy(qk_col[:], p_qk[:]),
                   reads=[("p_qk",)], writes=[("qk_col",)])
            P.emit("sync", lambda e: e.dma_start(
                out=s_vf[:],
                in_=vvf_d.rearrange("a b -> (a b)").rearrange("(o f) -> o f", o=1)),
                reads=[("vvf_d",)], writes=[("s_vf",)], inc=(INID, 16), dma=True, serialize=True)
            for cb in range(8):
                P.emit("tensor", lambda e, cb=cb: e.matmul(
                    p_vb[:], s_one[0:1, :], s_vf[0:1, cb * 512:(cb + 1) * 512],
                    start=True, stop=True),
                    reads=[("s_one",), ("s_vf",)], writes=[("p_vb",)], inc=(TC, 1))
                P.emit("vector", lambda e, cb=cb: e.tensor_copy(
                    v_bc[:, cb * 512:(cb + 1) * 512], p_vb[:]),
                    reads=[("p_vb",)], writes=[("v_bc", cb)])

        init_done = [(INID, INID.n), (VC, VC.n), (SC, SC.n)]

        own = [4 * j + 3 for j in range(NQT)]
        rest = [i for i in range(NROW) if i not in own]
        order = own + rest

        ffn_live = ExitStack()
        xn_own = ffn_live.enter_context(nc.sbuf_tensor("xn_own", [128, 32 * 1024], BF16))
        wu_b = [ffn_live.enter_context(nc.sbuf_tensor(f"wu_b{i}", [128, D], BF16)) for i in range(2)]
        wg_b = [ffn_live.enter_context(nc.sbuf_tensor(f"wg_b{i}", [128, D], BF16)) for i in range(2)]
        sg_t = [ffn_live.enter_context(nc.sbuf_tensor(f"sg_t{i}", [128, 1024], F32)) for i in range(2)]
        h_b = [ffn_live.enter_context(nc.sbuf_tensor(f"h_b{i}", [128, 1024], BF16)) for i in range(2)]
        p_upA = ffn_live.enter_context(nc.psum_tensor("p_up0", [128, 1024], F32))
        p_gtA = ffn_live.enter_context(nc.psum_tensor("p_gt0", [128, 1024], F32))

        ln_stack = ExitStack()
        x_b = [ln_stack.enter_context(nc.sbuf_tensor(f"x_b{i}", [128, D], F32)) for i in range(2)]
        xn_b = [ln_stack.enter_context(nc.sbuf_tensor(f"xn_b{i}", [128, D], BF16)) for i in range(2)]
        xt_b = [ln_stack.enter_context(nc.sbuf_tensor(f"xt_b{i}", [128, D], BF16)) for i in range(2)]
        dump = xn_b[0]
        st2 = ln_stack.enter_context(nc.sbuf_tensor("st2", [128, 256], F32))
        p_tr = [ln_stack.enter_context(nc.psum_tensor(f"p_tr{i}", [128, 512], F32)) for i in range(2)]

        def emit_pass1(n, ti):
            par = n % 2
            xb = x_b[par]
            xk = ("x_b", par)
            P.emit("sync", lambda e, ti=ti, xb=xb: e.dma_start(
                out=xb[:], in_=x_d[ti * 128:(ti + 1) * 128, :]),
                writes=[xk], extra_waits=(init_done if n < 2 else ()),
                inc=(XD[par], 16), dma=True)
            P.emit("vector", lambda e, xb=xb, n=n: e.reduce_sum(
                st2[:, n:n + 1], xb[:], AXX), reads=[xk], writes=[("st2s", n)])
            P.emit("scalar", lambda e, xb=xb, n=n: e.activation(
                dump[:], xb[:], AF.Square, accum_out=st2[:, 32 + n:33 + n]),
                reads=[xk], writes=[("xn_b", 0), ("st2q", n)])

        def emit_chain(c0, c1):
            sq = ([("st2s", t) for t in range(c0, c1)] +
                  [("st2q", t) for t in range(c0, c1)])
            ck = ("st2c", c0)
            P.emit("vector", lambda e, c0=c0, c1=c1: e.tensor_scalar(
                st2[:, 64 + c0:64 + c1], st2[:, c0:c1], -1.0 / D, None, ALU.mult),
                reads=sq, writes=[ck])
            P.emit("vector", lambda e, c0=c0, c1=c1: e.tensor_scalar(
                st2[:, 96 + c0:96 + c1], st2[:, 32 + c0:32 + c1], 1.0 / D, None, ALU.mult),
                reads=[ck], writes=[ck])
            P.emit("vector", lambda e, c0=c0, c1=c1: e.tensor_mul(
                st2[:, 128 + c0:128 + c1], st2[:, 64 + c0:64 + c1], st2[:, 64 + c0:64 + c1]),
                reads=[ck], writes=[ck])
            P.emit("vector", lambda e, c0=c0, c1=c1: e.tensor_sub(
                st2[:, 160 + c0:160 + c1], st2[:, 96 + c0:96 + c1], st2[:, 128 + c0:128 + c1]),
                reads=[ck], writes=[ck])
            P.emit("scalar", lambda e, c0=c0, c1=c1: e.activation(
                st2[:, 96 + c0:96 + c1], st2[:, 160 + c0:160 + c1], AF.Sqrt, bias=c_cst[:, 0:1]),
                reads=[ck, ("c_cst",)], writes=[ck])
            P.emit("vector", lambda e, c0=c0, c1=c1: e.reciprocal(
                st2[:, 192 + c0:192 + c1], st2[:, 96 + c0:96 + c1]),
                reads=[ck], writes=[ck])
            P.emit("vector", lambda e, c0=c0, c1=c1: e.tensor_mul(
                st2[:, 224 + c0:224 + c1], st2[:, 64 + c0:64 + c1], st2[:, 192 + c0:192 + c1]),
                reads=[ck], writes=[ck])

        def emit_pass2(n, ti):
            par = n % 2
            tpr = n % 2
            xb, xnb, xtb = x_b[par], xn_b[par], xt_b[tpr]
            xk, xnk, xtk = ("x_b", par), ("xn_b", par), ("xt_b", tpr)
            ck = ("st2c", 0 if n < 8 else 8)
            P.emit("sync", lambda e, ti=ti, xb=xb: e.dma_start(
                out=xb[:], in_=x_d[ti * 128:(ti + 1) * 128, :]),
                writes=[xk], inc=(XD[par], 16), dma=True)
            P.emit("scalar", lambda e, xb=xb, xnb=xnb, n=n: e.activation(
                xnb[:], xb[:], AF.Identity,
                bias=st2[:, 224 + n:225 + n], scale=st2[:, 192 + n:193 + n]),
                reads=[xk, ck], writes=[xnk])
            P.emit("sync", lambda e, ti=ti, xnb=xnb: e.dma_start(
                out=xna_d[ti * 128:(ti + 1) * 128, :], in_=xnb[:]),
                reads=[xnk], writes=[("xna", ti)], inc=(XA[par], 16), dma=True)
            for g in range(8):
                tpar = g % 2
                for u in range(4):
                    dj = 4 * g + u
                    P.emit("tensor", lambda e, xnb=xnb, dj=dj, u=u, tpar=tpar: e.matmul(
                        p_tr[tpar][:, u * 128:(u + 1) * 128],
                        xnb[:, dj * 128:(dj + 1) * 128], c_idn[:],
                        start=True, stop=True),
                        reads=[xnk, ("c_idn",)], writes=[("p_tr", tpar)],
                        inc=((TPS, 1) if u == 3 else None))
                P.emit("vector", lambda e, xtb=xtb, g=g, tpar=tpar: e.tensor_copy(
                    xtb[:, g * 512:(g + 1) * 512], p_tr[tpar][:]),
                    reads=[("p_tr", tpar)], writes=[xtk])
                if g % 2 == 1:
                    q = g // 2
                    P.emit("sync", lambda e, ti=ti, xtb=xtb, q=q: e.dma_start(
                        out=xnt_d[q * 1024:(q + 1) * 1024, ti * 128:(ti + 1) * 128]
                            .rearrange("(j p) t -> p j t", p=128),
                        in_=xtb[:, q * 1024:(q + 1) * 1024]
                            .rearrange("p (j t) -> p j t", t=128)),
                        reads=[xtk], writes=[("xnt", ti, q)],
                        inc=(XSS[tpr][q], 16), dma=True)

        for n in range(NQT):
            emit_pass1(n, order[n])
        emit_chain(0, NQT)
        for n in range(NQT):
            emit_pass2(n, order[n])

        for dj in range(32):
            P.emit("sync", lambda e, dj=dj: e.dma_start(
                out=xn_own[:, dj * 1024:(dj + 1) * 1024].rearrange("p (j t) -> p j t", t=128),
                in_=xnt_d[dj * 128:(dj + 1) * 128, :]
                    .rearrange("p (a f t) -> p a f t", f=4, t=128)[:, :, 3, :]),
                reads=[("xnt", t, dj // 8) for t in own], writes=[("xn_own", dj)],
                inc=(XO[dj % 4], 16), dma=True)

        phA = ExitStack()
        if True:
            NSB = 50
            p_up = [p_upA, None]
            p_gt = [p_gtA, None]

            def parb(ft):
                return 0 if ft < NSB else ft % 2

            def ffn_a_ft(ft):
                if ft < NROW - NQT:
                    emit_pass1(NQT + ft, order[NQT + ft])
                if ft == NROW - NQT:
                    emit_chain(NQT, NROW)
                if NROW - NQT < ft <= 2 * (NROW - NQT):
                    emit_pass2(NQT + ft - (NROW - NQT) - 1, order[NQT + ft - (NROW - NQT) - 1])

                par = ft % 2
                P.emit("gpsimd", lambda e, ft=ft, par=par: e.dma_start(
                    out=wg_b[par][:], in_=wg_d[ft]),
                    writes=[("wg_b", par)], extra_waits=(init_done if ft == 0 else ()),
                    inc=(WGS[par], 16), dma=True)
                P.emit("gpsimd", lambda e, ft=ft, par=par: e.dma_start(
                    out=wu_b[par][:], in_=wu_d[ft]),
                    writes=[("wu_b", par)], inc=(WUS[par], 16), dma=True)
                pb = parb(ft)
                xo_w = [(sem, sem.n) for sem in XO] if ft == 0 else []
                if ft == NSB:
                    xo_w = xo_w + [(VC, VC.n), (TPS, TPS.n)]
                for half in range(2):
                    for dj in range(32):
                        P.emit("tensor", lambda e, pb=pb, par=par, half=half, dj=dj: e.matmul(
                            p_gt[pb][:, half * 512:(half + 1) * 512],
                            wg_b[par][:, dj * 128:(dj + 1) * 128],
                            xn_own[:, dj * 1024 + half * 512: dj * 1024 + (half + 1) * 512],
                            start=(dj == 0), stop=(dj == 31)),
                            reads=[("wg_b", par), ("xn_own", dj)],
                            writes=[("p_gt", pb)], extra_waits=xo_w,
                            inc=((GT, 1) if (half == 1 and dj == 31) else None))
                for half in range(2):
                    for dj in range(32):
                        P.emit("tensor", lambda e, pb=pb, par=par, half=half, dj=dj: e.matmul(
                            p_up[pb][:, half * 512:(half + 1) * 512],
                            wu_b[par][:, dj * 128:(dj + 1) * 128],
                            xn_own[:, dj * 1024 + half * 512: dj * 1024 + (half + 1) * 512],
                            start=(dj == 0), stop=(dj == 31)),
                            reads=[("wu_b", par), ("xn_own", dj)],
                            writes=[("p_up", pb)],
                            inc=((UPS, 1) if (half == 1 and dj == 31) else None))
                P.emit("scalar", lambda e, pb=pb, par=par: e.activation(
                    sg_t[par][:], p_gt[pb][:], AF.Silu),
                    reads=[("p_gt", pb)], writes=[("sg_t", par)])
                P.emit("vector", lambda e, pb=pb, par=par: e.tensor_mul(
                    h_b[par][:], sg_t[par][:], p_up[pb][:]),
                    reads=[("sg_t", par), ("p_up", pb)], writes=[("h_b", par)])
                P.emit("sync", lambda e, ft=ft, par=par: e.dma_start(
                    out=h_d[ft], in_=h_b[par][:]),
                    reads=[("h_b", par)], writes=[("h_s", ft)], inc=(HDS[par], 16), dma=True)

            for ft in range(NSB):
                ffn_a_ft(ft)
            ln_mem = ([(sem, sem.n) for row in XSS for sem in row] +
                      [(sem, sem.n) for sem in XA] + [(sem, sem.n) for sem in XD])
            ln_stack.close()
            p_up[1] = phA.enter_context(nc.psum_tensor("p_up1", [128, 1024], F32))
            p_gt[1] = phA.enter_context(nc.psum_tensor("p_gt1", [128, 1024], F32))
            for ft in range(NSB, NFT):
                ffn_a_ft(ft)
            phA.close()

        ffna_done = [(sem, sem.n) for sem in HDS]
        ffn_live.close()

        with ExitStack() as ph:
            qf = ph.enter_context(nc.sbuf_tensor("qf", [128, 8 * 1024], BF16))
            pt = ph.enter_context(nc.sbuf_tensor("pt", [128, 32 * 256], BF16))
            vbuf = ph.enter_context(nc.sbuf_tensor("vbuf", [128, 32 * 1024], BF16))
            kvf = ph.enter_context(nc.sbuf_tensor("kvf", [128, 8 * 4096], BF16))
            s_o = [ph.enter_context(nc.sbuf_tensor(f"s_o{i}", [128, 1024], F32)) for i in range(2)]
            pool = [ph.enter_context(nc.psum_tensor(f"pp{i}", [128, 512], F32)) for i in range(8)]

            for h in range(NH):
                first = (ffna_done + ln_mem) if h == 0 else []
                for dj in range(8):
                    P.emit("sync", lambda e, h=h, dj=dj: e.dma_start(
                        out=qf[:, dj * 1024:(dj + 1) * 1024].rearrange("p (a t) -> p a t", t=128),
                        in_=xnt_d[h * 1024 + dj * 128: h * 1024 + (dj + 1) * 128, :]
                            .rearrange("p (a f t) -> p a f t", f=4, t=128)[:, :, 3, :]),
                        reads=[("xnt", t, (h * 1024 + dj * 128) // 1024) for t in own],
                        writes=[("qf", dj)], extra_waits=first, inc=(QSD, 16),
                        dma=True, serialize=True)
                    P.emit("vector", lambda e, h=h, dj=dj: e.tensor_scalar(
                        qf[:, dj * 1024:(dj + 1) * 1024],
                        qf[:, dj * 1024:(dj + 1) * 1024],
                        qk_col[:, 8 * h + dj: 8 * h + dj + 1], None, ALU.mult),
                        reads=[("qf", dj), ("qk_col",)], writes=[("qf", dj)])
                for jq in range(4):
                    P.emit("gpsimd", lambda e, h=h, jq=jq: e.dma_start(
                        out=kvf[:, jq * 8192:(jq + 1) * 8192].rearrange("p (dj s2) -> p dj s2", dj=2),
                        in_=xnt_d[h * 1024 + jq * 256: h * 1024 + (jq + 1) * 256, :]
                            .rearrange("(dj p) s2 -> p dj s2", p=128)),
                        reads=[("xnt", t, h) for t in range(NROW)],
                        writes=[("kvf", 2 * jq), ("kvf", 2 * jq + 1)],
                        extra_waits=first, inc=(KVS[jq], 16), dma=True)
                for qd in range(4):
                    P.emit("sync", lambda e, h=h, qd=qd: e.dma_start(
                        out=vbuf[:, qd * 8192:(qd + 1) * 8192].rearrange("p (a d) -> p a d", d=1024),
                        in_=xna_d[qd * 1024:(qd + 1) * 1024, h * 1024:(h + 1) * 1024]
                            .rearrange("(a p) d -> p a d", p=128)),
                        reads=[("xna", 8 * qd + i) for i in range(8)],
                        writes=[("vbuf", 8 * qd + i) for i in range(8)],
                        extra_waits=first, inc=(VLS[qd], 16), dma=True)
                for p in range(4):
                    n_chunk = 2 * p + 2
                    for ch in range(n_chunk):
                        cset = 4 * (ch % 2)
                        for dj in range(8):
                            for sj in range(4):
                                P.emit("tensor", lambda e, ch=ch, cset=cset, sj=sj, dj=dj, p=p:
                                       e.matmul(
                                        pool[cset + sj][:, 0:256],
                                        kvf[:, dj * 4096 + ch * 512 + sj * 128:
                                              dj * 4096 + ch * 512 + (sj + 1) * 128],
                                        qf[:, dj * 1024 + 256 * p: dj * 1024 + 256 * (p + 1)],
                                        start=(dj == 0), stop=(dj == 7)),
                                        reads=[("kvf", dj), ("qf", dj)],
                                        writes=[("pool", cset + sj)],
                                        inc=((KVF, 1) if sj == 3 else None))
                        masked = (ch >= n_chunk - 2)
                        for sj in range(4):
                            stile = 4 * ch + sj
                            P.emit("scalar", lambda e, cset=cset, sj=sj, stile=stile: e.activation(
                                pt[:, stile * 256:(stile + 1) * 256],
                                pool[cset + sj][:, 0:256], AF.Sigmoid),
                                reads=[("pool", cset + sj)], writes=[("pt", stile)])
                            if masked:
                                r = stile - 8 * p
                                P.emit("vector", lambda e, stile=stile, r=r: e.tensor_mul(
                                    pt[:, stile * 256:(stile + 1) * 256],
                                    pt[:, stile * 256:(stile + 1) * 256],
                                    c_msk[:, r * 256:(r + 1) * 256]),
                                    reads=[("pt", stile), ("c_msk",)], writes=[("pt", stile)])
                    for jh in range(2):
                        n_st = 8 * p + 4 * (jh + 1)
                        j = 2 * p + jh
                        for stile in range(n_st):
                            for cb in range(2):
                                P.emit("tensor", lambda e, jh=jh, stile=stile, cb=cb, n_st=n_st: e.matmul(
                                    pool[2 * jh + cb][:],
                                    pt[:, stile * 256 + jh * 128: stile * 256 + jh * 128 + 128],
                                    vbuf[:, stile * 1024 + cb * 512: stile * 1024 + (cb + 1) * 512],
                                    start=(stile == 0), stop=(stile == n_st - 1)),
                                    reads=[("pt", stile), ("vbuf", stile)],
                                    writes=[("pool", 2 * jh + cb)],
                                    inc=((PVD, 1) if (stile == n_st - 1 and cb == 1) else None))
                        P.emit("vector", lambda e, jh=jh, h=h: e.tensor_mul(
                            s_o[jh][:, 0:512], pool[2 * jh][:],
                            v_bc[:, h * 1024:h * 1024 + 512]),
                            reads=[("pool", 2 * jh), ("v_bc", 2 * h)],
                            writes=[("s_o", jh, 0)])
                        P.emit("vector", lambda e, jh=jh, h=h: e.tensor_mul(
                            s_o[jh][:, 512:1024], pool[2 * jh + 1][:],
                            v_bc[:, h * 1024 + 512:(h + 1) * 1024]),
                            reads=[("pool", 2 * jh + 1), ("v_bc", 2 * h + 1)],
                            writes=[("s_o", jh, 1)], inc=(SEV, 1))
                        P.emit("sync", lambda e, j=j, h=h, jh=jh: e.dma_start(
                            out=s_d[j * 128:(j + 1) * 128, h * 1024:(h + 1) * 1024],
                            in_=s_o[jh][:]),
                            reads=[("s_o", jh, 0), ("s_o", jh, 1)],
                            writes=[("s_s", j, h)], inc=(SDS[jh], 16), dma=True)

        att_done = ([(PVD, PVD.n), (SEV, SEV.n)] +
                    [(sem, sem.n) for sem in SDS] + [(sem, sem.n) for sem in KVS])
        att_live.close()

        with ExitStack() as ph:
            h_sb = ph.enter_context(nc.sbuf_tensor("h_sb", [128, NFT * 1024], BF16))
            wd_b = [ph.enter_context(nc.sbuf_tensor(f"wd_b{i}", [128, 1024], BF16)) for i in range(3)]
            g_sb = [ph.enter_context(nc.sbuf_tensor(f"g_sb{i}", [128, 512], F32)) for i in range(3)]
            s_t = [ph.enter_context(nc.sbuf_tensor(f"s_t{i}", [128, 512], F32)) for i in range(3)]
            x_t = [ph.enter_context(nc.sbuf_tensor(f"x_t{i}", [128, 512], F32)) for i in range(3)]
            o_t = [ph.enter_context(nc.sbuf_tensor(f"o_t{i}", [128, 512], F32)) for i in range(3)]
            p_y = [ph.enter_context(nc.psum_tensor(f"p_y{i}", [128, 512], F32)) for i in range(8)]

            for ft in range(NFT):
                qeng = "sync" if ft % 2 == 0 else "scalar"
                qsem = HLD if ft % 2 == 0 else HLD2
                P.emit(qeng, lambda e, ft=ft: e.dma_start(
                    out=h_sb[:, ft * 1024:(ft + 1) * 1024], in_=h_d[ft]),
                    reads=[("h_s", ft)], writes=[("h_sb",)],
                    extra_waits=(att_done if ft < 2 else ()), inc=(qsem, 16), dma=True)

            for db in range(NDB):
                for ftp in range(NFT // 2):
                    gslot = ((NFT // 2) * db + ftp) % 3
                    P.emit("gpsimd", lambda e, db=db, ftp=ftp, gslot=gslot: e.dma_start(
                        out=wd_b[gslot][:].rearrange("p (l c) -> p l c", c=512),
                        in_=wd_d[db, 2 * ftp:2 * ftp + 2].rearrange("l p c -> p l c")),
                        writes=[("wd_b", gslot)],
                        extra_waits=(att_done if (db == 0 and ftp == 0) else ()),
                        inc=(WDS[gslot], 16), dma=True)
                    wpar = gslot
                    for fl in range(2):
                        ft = 2 * ftp + fl
                        for tt in range(8):
                            inc = None
                            if fl == 1 and tt == 7:
                                inc = (WDF, 1)
                            elif ft == NFT - 1:
                                inc = (YDT, 1)
                            P.emit("tensor", lambda e, par=wpar, fl=fl, ft=ft, tt=tt: e.matmul(
                                p_y[tt][:],
                                h_sb[:, ft * 1024 + tt * 128: ft * 1024 + (tt + 1) * 128],
                                wd_b[par][:, fl * 512:(fl + 1) * 512],
                                start=(ft == 0), stop=(ft == NFT - 1)),
                                reads=[("wd_b", wpar), ("h_sb",)],
                                writes=[("p_y", tt)], inc=inc)
                def sx_load(db, tt):
                    sl = tt % 3
                    P.emit("sync", lambda e, db=db, tt=tt, sl=sl: e.dma_start(
                        out=s_t[sl][:],
                        in_=s_d[tt * 128:(tt + 1) * 128, db * 512:(db + 1) * 512]),
                        reads=[("s_s", tt, db // 2)],
                        writes=[("s_t", sl)], inc=(STS[sl], 16), dma=True)
                    P.emit("sync", lambda e, db=db, tt=tt, sl=sl: e.dma_start(
                        out=x_t[sl][:],
                        in_=x_d.rearrange("(a p) d2 -> a p d2", p=128)[4 * tt + 3][:, db * 512:(db + 1) * 512]),
                        writes=[("x_t", sl)], inc=(XTS[sl], 16), dma=True)

                sx_load(db, 0)
                sx_load(db, 1)
                for tt in range(8):
                    sl = tt % 3
                    if tt + 2 < 8:
                        sx_load(db, tt + 2)
                    P.emit("scalar", lambda e, tt=tt, sl=sl: e.activation(
                        g_sb[sl][:], p_y[tt][:], AF.Silu),
                        reads=[("p_y", tt)], writes=[("g_sb", sl)])
                    P.emit("vector", lambda e, tt=tt, sl=sl: e.tensor_mul(
                        g_sb[sl][:], g_sb[sl][:], s_t[sl][:]),
                        reads=[("g_sb", sl), ("s_t", sl)],
                        writes=[("g_sb", sl)])
                    P.emit("vector", lambda e, tt=tt, sl=sl: e.tensor_add(
                        o_t[sl][:], g_sb[sl][:], x_t[sl][:]),
                        reads=[("g_sb", sl), ("x_t", sl)],
                        writes=[("o_t", sl)])
                    P.emit("scalar", lambda e, db=db, tt=tt, sl=sl: e.dma_start(
                        out=out_d[tt * 128:(tt + 1) * 128, db * 512:(db + 1) * 512],
                        in_=o_t[sl][:]),
                        reads=[("o_t", sl)], writes=[("outp", tt, db)],
                        inc=(ODS[sl], 16), dma=True)

            P.emit("sync", lambda e: e.nop(),
                   extra_waits=[(sem, sem.n) for sem in ODS])

        with nc.Block() as block:
            P.build(block)

    return nc


_NC_CACHE = []


def kernel(L0_attn_x, L0_attn_bv_q, L0_attn_bv_k, L0_attn_bv_v,
           L0_ffn_W_gate, L0_ffn_W_up, L0_ffn_W_down):
    x = np.ascontiguousarray(np.asarray(L0_attn_x, dtype=np.float32))
    bq = np.ascontiguousarray(np.asarray(L0_attn_bv_q, dtype=np.float32))
    bk = np.ascontiguousarray(np.asarray(L0_attn_bv_k, dtype=np.float32))
    bv = np.ascontiguousarray(np.asarray(L0_attn_bv_v, dtype=np.float32))
    wg = np.asarray(L0_ffn_W_gate, dtype=np.float32)
    wu = np.asarray(L0_ffn_W_up, dtype=np.float32)
    wd = np.asarray(L0_ffn_W_down, dtype=np.float32)

    wu_t = np.ascontiguousarray(wu.reshape(NFT, 128, 32, 128).transpose(0, 3, 2, 1))
    wg_t = np.ascontiguousarray(wg.reshape(NFT, 128, 32, 128).transpose(0, 3, 2, 1))
    wd_t = np.ascontiguousarray(wd.reshape(NDB, 512, NFT, 128).transpose(0, 2, 3, 1))
    idn = np.eye(128).astype(ml_dtypes.bfloat16)
    i32 = np.eye(32).astype(np.float32)

    in_maps = []
    for c in range(NCORE):
        b, k = c // 4, c % 4
        delta = [t for t in range(4) if t != k] + [k]
        order = []
        for g in range(8):
            order += [4 * g + d for d in delta]
        xp = np.ascontiguousarray(x[b].reshape(32, 128, D)[order].reshape(32 * 128, D))
        m = np.zeros((8, 128, 256), np.float32)
        pr = np.arange(128)[:, None]
        pc = np.arange(128)[None, :]
        for r in range(8):
            so = 4 * (r // 4) + delta[r % 4]
            for hh in range(2):
                m[r][:, hh * 128:(hh + 1) * 128] = (
                    128 * so + pr <= 128 * (4 * hh + k) + pc)
        in_maps.append({
            "x": xp, "wu": wu_t, "wg": wg_t, "wd": wd_t,
            "bq": bq, "bk": bk, "bv": bv,
            "msk": m.astype(ml_dtypes.bfloat16), "idn": idn, "i32": i32,
        })

    if not _NC_CACHE:
        _NC_CACHE.append(build_program())
    nc = _NC_CACHE[0]
    res = run_bass_kernel_spmd(nc, in_maps, core_ids=list(range(NCORE)))

    out = np.empty((2, 4096, D), np.float32)
    for c in range(NCORE):
        b, k = c // 4, c % 4
        shard = np.asarray(res.results[c]["out"]).reshape(NQT, 128, D)
        ob = out[b].reshape(32, 128, D)
        for j in range(NQT):
            ob[4 * j + k] = shard[j]
    return out


if __name__ == "__main__":
    nc = build_program()
    print("built instructions:", nc.n_instructions())


# version 09
# speedup vs baseline: 1.0346x; 1.0346x over previous
import numpy as np
import ml_dtypes
from contextlib import ExitStack

import concourse.bass as bass
import concourse.mybir as mybir
from concourse.bass_utils import run_bass_kernel_spmd

F32 = mybir.dt.float32
BF16 = mybir.dt.bfloat16
AF = mybir.ActivationFunctionType
ALU = mybir.AluOpType
AXX = mybir.AxisListType.X

D = 4096
DFF = 11008
NROW = 32
NQT = 8
NFT = 86
NDB = 8
NH = 4
NCORE = 8

ENGINES = ("sync", "scalar", "vector", "tensor", "gpsimd")


class Sem:
    def __init__(self, handle, name):
        self.h = handle
        self.name = name
        self.n = 0


class Prog:

    def __init__(self, chain_sems):
        self.records = []
        self.chain = chain_sems
        self.last_writer = {}
        self.readers = {}
        self.rec_engine = []
        self.rec_dma = []
        self.rec_inc = []
        self.eng_incs = {e: [] for e in ENGINES}

    def _cover(self, idx):
        e = self.rec_engine[idx]
        if self.rec_dma[idx]:
            inc = self.rec_inc[idx]
            assert inc is not None, f"DMA record {idx} has no completion inc"
            return (inc[0], inc[2])
        for (ridx, sem, val) in self.eng_incs[e]:
            if ridx >= idx:
                return (sem, val)
        raise RuntimeError(f"no cover for record {idx} on engine {e}")

    def emit(self, eng, fn, reads=(), writes=(), inc=None, extra_waits=(),
             dma=False, auto_inc=True, serialize=False):
        idx = len(self.records)
        waits = {}
        if serialize and inc is not None:
            extra_waits = list(extra_waits) + [(inc[0], inc[0].n)]

        def add_wait(sem, val):
            if val > 0 and waits.get(id(sem), (None, -1))[1] < val:
                waits[id(sem)] = (sem, val)

        for sem, val in extra_waits:
            add_wait(sem, val)

        def dep(pidx):
            if self.rec_engine[pidx] == "tensor" and eng == "tensor" \
                    and not self.rec_dma[pidx] and not dma:
                return
            sem, val = self._cover(pidx)
            add_wait(sem, val)

        for key in reads:
            w = self.last_writer.get(key)
            if w is not None:
                dep(w)
        for key in writes:
            for r in self.readers.get(key, ()):
                dep(r)
            w = self.last_writer.get(key)
            if w is not None:
                dep(w)

        for key in reads:
            self.readers.setdefault(key, []).append(idx)
        for key in writes:
            self.last_writer[key] = idx
            self.readers[key] = []

        if inc is None and auto_inc and not dma and eng in ("vector", "scalar", "gpsimd"):
            inc = (self.chain[eng], 1)
        rec_inc = None
        if inc is not None:
            sem, amt = inc
            sem.n += amt
            rec_inc = (sem, amt, sem.n)
            self.eng_incs[eng].append((idx, sem, sem.n))

        self.records.append((eng, fn, sorted(waits.values(), key=lambda t: id(t[0])), rec_inc))
        self.rec_engine.append(eng)
        self.rec_dma.append(dma)
        self.rec_inc.append(rec_inc)
        return idx

    def build(self, block):
        last_wait = {e: {} for e in ENGINES}
        stats = {e: [0, 0, 0] for e in ENGINES}

        def run_engine(name, eng):
            lw = last_wait[name]
            for (rec_eng, fn, waits, inc) in self.records:
                if rec_eng != name:
                    continue
                stats[name][0] += 1
                for sem, val in waits:
                    if lw.get(id(sem), -1) >= val:
                        stats[name][2] += 1
                        continue
                    eng.wait_ge(sem.h, val)
                    lw[id(sem)] = val
                    stats[name][1] += 1
                ins = fn(eng)
                if inc is not None:
                    ins.then_inc(inc[0].h, inc[1])

        @block.sync
        def _(e):
            run_engine("sync", e)

        @block.scalar
        def _(e):
            run_engine("scalar", e)

        @block.vector
        def _(e):
            run_engine("vector", e)

        @block.tensor
        def _(e):
            run_engine("tensor", e)

        @block.gpsimd
        def _(e):
            run_engine("gpsimd", e)

        return stats


def build_program():
    nc = bass.Bass("TRN2", target_bir_lowering=False, debug=False)

    x_d = nc.dram_tensor("x", [NROW * 128, D], F32, kind="ExternalInput").ap()
    wu_d = nc.dram_tensor("wu", [NFT, 128, D], F32, kind="ExternalInput").ap()
    wg_d = nc.dram_tensor("wg", [NFT, 128, D], F32, kind="ExternalInput").ap()
    wd_d = nc.dram_tensor("wd", [NDB, NFT, 128, 512], F32, kind="ExternalInput").ap()
    bq_d = nc.dram_tensor("bq", [4, 1024], F32, kind="ExternalInput").ap()
    bk_d = nc.dram_tensor("bk", [4, 1024], F32, kind="ExternalInput").ap()
    bv_d = nc.dram_tensor("bv", [4, 1024], F32, kind="ExternalInput").ap()
    msk_d = nc.dram_tensor("msk", [8, 128, 256], BF16, kind="ExternalInput").ap()
    idn_d = nc.dram_tensor("idn", [128, 128], BF16, kind="ExternalInput").ap()
    i32_d = nc.dram_tensor("i32", [32, 32], F32, kind="ExternalInput").ap()
    out_d = nc.dram_tensor("out", [1024, D], F32, kind="ExternalOutput").ap()

    xnt_d = nc.dram_tensor("xnt_s", [D, NROW * 128], BF16, kind="Internal").ap()
    xna_d = nc.dram_tensor("xna_s", [NROW * 128, D], BF16, kind="Internal").ap()
    h_d = nc.dram_tensor("h_s", [NFT, 128, 1024], BF16, kind="Internal").ap()
    s_d = nc.dram_tensor("s_s", [1024, D], F32, kind="Internal").ap()
    qkv_d = nc.dram_tensor("qkv_s", [4, 1024], F32, kind="Internal").ap()
    vvf_d = nc.dram_tensor("vvf_s", [4, 1024], F32, kind="Internal").ap()

    st = ExitStack()

    def mksem(name):
        return Sem(st.enter_context(nc.semaphore(name)), name)

    with st:
        VC, SC, GC = mksem("VC"), mksem("SC"), mksem("GC")
        TC = mksem("TC")
        INID = mksem("INID")
        XD = [mksem(f"XD{i}") for i in range(3)]
        XSS = [[mksem(f"XS{i}_{q}") for q in range(4)] for i in range(2)]
        XA = [mksem(f"XA{i}") for i in range(3)]
        XO = [mksem(f"XO{i}") for i in range(4)]
        WGS = [mksem(f"WG{i}") for i in range(2)]
        WUS = [mksem(f"WU{i}") for i in range(2)]
        GT = mksem("GT")
        UPS = mksem("UPS")
        HDS = [mksem(f"HD{i}") for i in range(2)]
        VLS = [mksem(f"VL{i}") for i in range(4)]
        QSD = mksem("QSD")
        KVS = [mksem(f"KV{i}") for i in range(4)]
        KVF = mksem("KVF")
        TPS = mksem("TPS")
        PVD = mksem("PVD")
        SEV = mksem("SEV")
        SDS = [mksem(f"SD{i}") for i in range(2)]
        HLD = mksem("HLD")
        HLD2 = mksem("HLD2")
        WDS = [mksem(f"WD{i}") for i in range(3)]
        WDF = mksem("WDF")
        YDT = mksem("YDT")
        ODS = [mksem(f"OD{i}") for i in range(3)]
        STS = [mksem(f"ST{i}") for i in range(3)]
        XTS = [mksem(f"XT{i}") for i in range(3)]

        chain = {"vector": VC, "scalar": SC, "gpsimd": GC}
        P = Prog(chain)

        c_cst = st.enter_context(nc.sbuf_tensor("c_cst", [128, 4], F32))
        c_idn = st.enter_context(nc.sbuf_tensor("c_idn", [128, 128], BF16))

        att_live = ExitStack()
        qk_col = att_live.enter_context(nc.sbuf_tensor("qk_col", [128, 32], F32))
        v_bc = att_live.enter_context(nc.sbuf_tensor("v_bc", [128, D], F32))
        c_msk = att_live.enter_context(nc.sbuf_tensor("c_msk", [128, 8 * 256], BF16))

        with ExitStack() as ph:
            s_bq = ph.enter_context(nc.sbuf_tensor("s_bq", [4, 1024], F32))
            s_bk = ph.enter_context(nc.sbuf_tensor("s_bk", [4, 1024], F32))
            s_bv = ph.enter_context(nc.sbuf_tensor("s_bv", [4, 1024], F32))
            s_al = ph.enter_context(nc.sbuf_tensor("s_al", [4, 4], F32))
            s_tmp = ph.enter_context(nc.sbuf_tensor("s_tmp", [4, 1024], F32))
            s_i32 = ph.enter_context(nc.sbuf_tensor("s_i32", [32, 32], F32))
            s_q32 = ph.enter_context(nc.sbuf_tensor("s_q32", [32, 128], F32))
            s_vf = ph.enter_context(nc.sbuf_tensor("s_vf", [1, D], F32))
            s_one = ph.enter_context(nc.sbuf_tensor("s_one", [1, 128], F32))
            p_qk = ph.enter_context(nc.psum_tensor("p_qk", [128, 32], F32))
            p_vb = ph.enter_context(nc.psum_tensor("p_vb", [128, 512], F32))

            bufs = {"s_bq": s_bq, "s_bk": s_bk, "s_bv": s_bv}
            for dst, src, key in [(c_idn, idn_d, "c_idn"), (s_i32, i32_d, "s_i32"),
                                  (s_bq, bq_d, "s_bq"), (s_bk, bk_d, "s_bk"),
                                  (s_bv, bv_d, "s_bv")]:
                P.emit("sync", lambda e, dst=dst, src=src: e.dma_start(
                    out=dst[:], in_=src), writes=[(key,)], inc=(INID, 16), dma=True, serialize=True)
            P.emit("sync", lambda e: e.dma_start(
                out=c_msk[:].rearrange("p (r t) -> p r t", r=8),
                in_=msk_d.rearrange("r p t -> p r t")),
                writes=[("c_msk",)], inc=(INID, 16), dma=True, serialize=True)
            P.emit("vector", lambda e: e.memset(c_cst[:, 0:1], 1e-5), writes=[("c_cst",)])
            P.emit("vector", lambda e: e.memset(c_cst[:, 1:2], 0.0), writes=[("c_cst",)])
            P.emit("vector", lambda e: e.memset(s_one[:, :], 1.0), writes=[("s_one",)])
            for col, key in ((0, "s_bq"), (1, "s_bk"), (2, "s_bv")):
                P.emit("vector", lambda e, s_w=bufs[key], col=col: e.reduce_sum(
                    s_al[:, col:col + 1], s_w[:], AXX, apply_absolute_value=True),
                    reads=[(key,)], writes=[("s_al",)])
            P.emit("scalar", lambda e: e.mul(s_al[:, 0:3], s_al[:, 0:3], 1.0 / 1024),
                   reads=[("s_al",)], writes=[("s_al",)])
            for key in ("s_bq", "s_bk", "s_bv"):
                P.emit("vector", lambda e, s_w=bufs[key]: e.tensor_scalar(
                    s_w[:], s_w[:], c_cst[0:4, 1:2], None, ALU.is_ge),
                    reads=[(key,), ("c_cst",)], writes=[(key,)])
                P.emit("vector", lambda e, s_w=bufs[key]: e.tensor_scalar(
                    s_w[:], s_w[:], 2.0, -1.0, ALU.mult, ALU.add),
                    reads=[(key,)], writes=[(key,)])
            P.emit("vector", lambda e: e.tensor_mul(s_tmp[:], s_bq[:], s_bk[:]),
                   reads=[("s_bq",), ("s_bk",)], writes=[("s_tmp",)])
            P.emit("vector", lambda e: e.tensor_scalar(
                s_tmp[:], s_tmp[:], s_al[:, 0:1], s_al[:, 1:2], ALU.mult, ALU.mult),
                reads=[("s_tmp",), ("s_al",)], writes=[("s_tmp",)])
            P.emit("vector", lambda e: e.tensor_scalar(
                s_tmp[:], s_tmp[:], 0.125, None, ALU.mult),
                reads=[("s_tmp",)], writes=[("s_tmp",)])
            P.emit("vector", lambda e: e.tensor_scalar(
                s_bv[:], s_bv[:], s_al[:, 2:3], None, ALU.mult),
                reads=[("s_bv",), ("s_al",)], writes=[("s_bv",)])
            P.emit("sync", lambda e: e.dma_start(out=qkv_d, in_=s_tmp[:]),
                   reads=[("s_tmp",)], writes=[("qkv_d",)], inc=(INID, 16), dma=True, serialize=True)
            P.emit("sync", lambda e: e.dma_start(out=vvf_d, in_=s_bv[:]),
                   reads=[("s_bv",)], writes=[("vvf_d",)], inc=(INID, 16), dma=True, serialize=True)
            P.emit("sync", lambda e: e.dma_start(
                out=s_q32[:],
                in_=qkv_d.rearrange("a b -> (a b)").rearrange("(dt p) -> dt p", p=128)),
                reads=[("qkv_d",)], writes=[("s_q32",)], inc=(INID, 16), dma=True, serialize=True)
            P.emit("tensor", lambda e: e.matmul(
                p_qk[:], s_q32[:], s_i32[:], start=True, stop=True),
                reads=[("s_q32",), ("s_i32",)], writes=[("p_qk",)], inc=(TC, 1))
            P.emit("vector", lambda e: e.tensor_copy(qk_col[:], p_qk[:]),
                   reads=[("p_qk",)], writes=[("qk_col",)])
            P.emit("sync", lambda e: e.dma_start(
                out=s_vf[:],
                in_=vvf_d.rearrange("a b -> (a b)").rearrange("(o f) -> o f", o=1)),
                reads=[("vvf_d",)], writes=[("s_vf",)], inc=(INID, 16), dma=True, serialize=True)
            for cb in range(8):
                P.emit("tensor", lambda e, cb=cb: e.matmul(
                    p_vb[:], s_one[0:1, :], s_vf[0:1, cb * 512:(cb + 1) * 512],
                    start=True, stop=True),
                    reads=[("s_one",), ("s_vf",)], writes=[("p_vb",)], inc=(TC, 1))
                P.emit("vector", lambda e, cb=cb: e.tensor_copy(
                    v_bc[:, cb * 512:(cb + 1) * 512], p_vb[:]),
                    reads=[("p_vb",)], writes=[("v_bc", cb)])

        init_done = [(INID, INID.n), (VC, VC.n), (SC, SC.n)]

        own = [4 * j + 3 for j in range(NQT)]
        rest = [i for i in range(NROW) if i not in own]
        order = own + rest

        ffn_live = ExitStack()
        xn_own = ffn_live.enter_context(nc.sbuf_tensor("xn_own", [128, 32 * 1024], BF16))
        wu_b = [ffn_live.enter_context(nc.sbuf_tensor(f"wu_b{i}", [128, D], BF16)) for i in range(2)]
        wg_b = [ffn_live.enter_context(nc.sbuf_tensor(f"wg_b{i}", [128, D], BF16)) for i in range(2)]
        sg_t = [ffn_live.enter_context(nc.sbuf_tensor(f"sg_t{i}", [128, 1024], F32)) for i in range(2)]
        h_b = [ffn_live.enter_context(nc.sbuf_tensor(f"h_b{i}", [128, 1024], BF16)) for i in range(2)]
        p_u = [ffn_live.enter_context(nc.psum_tensor(f"p_u{i}", [128, 512], F32)) for i in range(2)]
        p_g = [ffn_live.enter_context(nc.psum_tensor(f"p_g{i}", [128, 512], F32)) for i in range(2)]

        ln_stack = ExitStack()
        x_b = [ln_stack.enter_context(nc.sbuf_tensor(f"x_b{i}", [128, D], F32)) for i in range(2)]
        xn_b = [ln_stack.enter_context(nc.sbuf_tensor(f"xn_b{i}", [128, D], BF16)) for i in range(2)]
        xt_b = [ln_stack.enter_context(nc.sbuf_tensor(f"xt_b{i}", [128, D], BF16)) for i in range(2)]
        dump = xn_b[0]
        st2 = ln_stack.enter_context(nc.sbuf_tensor("st2", [128, 256], F32))
        p_tr = [ln_stack.enter_context(nc.psum_tensor(f"p_tr{i}", [128, 512], F32)) for i in range(2)]

        def emit_pass1(n, ti):
            par = n % 2
            xb = x_b[par]
            xk = ("x_b", par)
            P.emit("sync", lambda e, ti=ti, xb=xb: e.dma_start(
                out=xb[:], in_=x_d[ti * 128:(ti + 1) * 128, :]),
                writes=[xk], extra_waits=(init_done if n < 2 else ()),
                inc=(XD[par], 16), dma=True)
            P.emit("vector", lambda e, xb=xb, n=n: e.reduce_sum(
                st2[:, n:n + 1], xb[:], AXX), reads=[xk], writes=[("st2s", n)])
            P.emit("scalar", lambda e, xb=xb, n=n: e.activation(
                dump[:], xb[:], AF.Square, accum_out=st2[:, 32 + n:33 + n]),
                reads=[xk], writes=[("xn_b", 0), ("st2q", n)])

        def emit_chain(c0, c1):
            sq = ([("st2s", t) for t in range(c0, c1)] +
                  [("st2q", t) for t in range(c0, c1)])
            ck = ("st2c", c0)
            P.emit("vector", lambda e, c0=c0, c1=c1: e.tensor_scalar(
                st2[:, 64 + c0:64 + c1], st2[:, c0:c1], -1.0 / D, None, ALU.mult),
                reads=sq, writes=[ck])
            P.emit("vector", lambda e, c0=c0, c1=c1: e.tensor_scalar(
                st2[:, 96 + c0:96 + c1], st2[:, 32 + c0:32 + c1], 1.0 / D, None, ALU.mult),
                reads=[ck], writes=[ck])
            P.emit("vector", lambda e, c0=c0, c1=c1: e.tensor_mul(
                st2[:, 128 + c0:128 + c1], st2[:, 64 + c0:64 + c1], st2[:, 64 + c0:64 + c1]),
                reads=[ck], writes=[ck])
            P.emit("vector", lambda e, c0=c0, c1=c1: e.tensor_sub(
                st2[:, 160 + c0:160 + c1], st2[:, 96 + c0:96 + c1], st2[:, 128 + c0:128 + c1]),
                reads=[ck], writes=[ck])
            P.emit("scalar", lambda e, c0=c0, c1=c1: e.activation(
                st2[:, 96 + c0:96 + c1], st2[:, 160 + c0:160 + c1], AF.Sqrt, bias=c_cst[:, 0:1]),
                reads=[ck, ("c_cst",)], writes=[ck])
            P.emit("vector", lambda e, c0=c0, c1=c1: e.reciprocal(
                st2[:, 192 + c0:192 + c1], st2[:, 96 + c0:96 + c1]),
                reads=[ck], writes=[ck])
            P.emit("vector", lambda e, c0=c0, c1=c1: e.tensor_mul(
                st2[:, 224 + c0:224 + c1], st2[:, 64 + c0:64 + c1], st2[:, 192 + c0:192 + c1]),
                reads=[ck], writes=[ck])

        def emit_pass2(n, ti):
            par = n % 2
            tpr = n % 2
            xb, xnb, xtb = x_b[par], xn_b[par], xt_b[tpr]
            xk, xnk, xtk = ("x_b", par), ("xn_b", par), ("xt_b", tpr)
            ck = ("st2c", 0 if n < 8 else 8)
            P.emit("sync", lambda e, ti=ti, xb=xb: e.dma_start(
                out=xb[:], in_=x_d[ti * 128:(ti + 1) * 128, :]),
                writes=[xk], inc=(XD[par], 16), dma=True)
            P.emit("scalar", lambda e, xb=xb, xnb=xnb, n=n: e.activation(
                xnb[:], xb[:], AF.Identity,
                bias=st2[:, 224 + n:225 + n], scale=st2[:, 192 + n:193 + n]),
                reads=[xk, ck], writes=[xnk])
            P.emit("sync", lambda e, ti=ti, xnb=xnb: e.dma_start(
                out=xna_d[ti * 128:(ti + 1) * 128, :], in_=xnb[:]),
                reads=[xnk], writes=[("xna", ti)], inc=(XA[par], 16), dma=True)
            for g in range(8):
                tpar = g % 2
                for u in range(4):
                    dj = 4 * g + u
                    P.emit("tensor", lambda e, xnb=xnb, dj=dj, u=u, tpar=tpar: e.matmul(
                        p_tr[tpar][:, u * 128:(u + 1) * 128],
                        xnb[:, dj * 128:(dj + 1) * 128], c_idn[:],
                        start=True, stop=True),
                        reads=[xnk, ("c_idn",)], writes=[("p_tr", tpar)],
                        inc=((TPS, 1) if u == 3 else None))
                P.emit("vector", lambda e, xtb=xtb, g=g, tpar=tpar: e.tensor_copy(
                    xtb[:, g * 512:(g + 1) * 512], p_tr[tpar][:]),
                    reads=[("p_tr", tpar)], writes=[xtk])
                if g % 2 == 1:
                    q = g // 2
                    P.emit("sync", lambda e, ti=ti, xtb=xtb, q=q: e.dma_start(
                        out=xnt_d[q * 1024:(q + 1) * 1024, ti * 128:(ti + 1) * 128]
                            .rearrange("(j p) t -> p j t", p=128),
                        in_=xtb[:, q * 1024:(q + 1) * 1024]
                            .rearrange("p (j t) -> p j t", t=128)),
                        reads=[xtk], writes=[("xnt", ti, q)],
                        inc=(XSS[tpr][q], 16), dma=True)

        for n in range(NQT):
            emit_pass1(n, order[n])
        emit_chain(0, NQT)
        for n in range(NQT):
            emit_pass2(n, order[n])

        for dj in range(32):
            P.emit("sync", lambda e, dj=dj: e.dma_start(
                out=xn_own[:, dj * 1024:(dj + 1) * 1024].rearrange("p (j t) -> p j t", t=128),
                in_=xnt_d[dj * 128:(dj + 1) * 128, :]
                    .rearrange("p (a f t) -> p a f t", f=4, t=128)[:, :, 3, :]),
                reads=[("xnt", t, dj // 8) for t in own], writes=[("xn_own", dj)],
                inc=(XO[dj % 4], 16), dma=True)

        if True:
            def ffn_a_ft(ft):
                par = ft % 2
                P.emit("gpsimd", lambda e, ft=ft, par=par: e.dma_start(
                    out=wg_b[par][:], in_=wg_d[ft]),
                    writes=[("wg_b", par)], extra_waits=(init_done if ft == 0 else ()),
                    inc=(WGS[par], 16), dma=True)
                P.emit("gpsimd", lambda e, ft=ft, par=par: e.dma_start(
                    out=wu_b[par][:], in_=wu_d[ft]),
                    writes=[("wu_b", par)], inc=(WUS[par], 16), dma=True)
                xo_w = [(sem, sem.n) for sem in XO] if ft == 0 else []
                for half in range(2):
                    hp = (2 * ft + half) % 2
                    for dj in range(32):
                        P.emit("tensor", lambda e, hp=hp, par=par, half=half, dj=dj: e.matmul(
                            p_g[hp][:],
                            wg_b[par][:, dj * 128:(dj + 1) * 128],
                            xn_own[:, dj * 1024 + half * 512: dj * 1024 + (half + 1) * 512],
                            start=(dj == 0), stop=(dj == 31)),
                            reads=[("wg_b", par), ("xn_own", dj)],
                            writes=[("p_g", hp)], extra_waits=(xo_w if (half == 0 and dj == 0) else []),
                            inc=((GT, 1) if dj == 31 else None))
                    for dj in range(32):
                        P.emit("tensor", lambda e, hp=hp, par=par, half=half, dj=dj: e.matmul(
                            p_u[hp][:],
                            wu_b[par][:, dj * 128:(dj + 1) * 128],
                            xn_own[:, dj * 1024 + half * 512: dj * 1024 + (half + 1) * 512],
                            start=(dj == 0), stop=(dj == 31)),
                            reads=[("wu_b", par), ("xn_own", dj)],
                            writes=[("p_u", hp)],
                            inc=((UPS, 1) if dj == 31 else None))
                    P.emit("scalar", lambda e, hp=hp, par=par, half=half: e.activation(
                        sg_t[par][:, half * 512:(half + 1) * 512], p_g[hp][:], AF.Silu),
                        reads=[("p_g", hp)], writes=[("sg_t", par, half)])
                    P.emit("vector", lambda e, hp=hp, par=par, half=half: e.tensor_mul(
                        h_b[par][:, half * 512:(half + 1) * 512],
                        sg_t[par][:, half * 512:(half + 1) * 512], p_u[hp][:]),
                        reads=[("sg_t", par, half), ("p_u", hp)], writes=[("h_b", par, half)])
                P.emit("sync", lambda e, ft=ft, par=par: e.dma_start(
                    out=h_d[ft], in_=h_b[par][:]),
                    reads=[("h_b", par, 0), ("h_b", par, 1)], writes=[("h_s", ft)],
                    inc=(HDS[par], 16), dma=True)

            for ft in range(NFT):
                ffn_a_ft(ft)
                if ft < NROW - NQT:
                    emit_pass1(NQT + ft, order[NQT + ft])
                if ft == NROW - NQT:
                    emit_chain(NQT, NROW)
                if NROW - NQT < ft <= 2 * (NROW - NQT):
                    emit_pass2(NQT + ft - (NROW - NQT) - 1, order[NQT + ft - (NROW - NQT) - 1])

        ln_mem = ([(sem, sem.n) for row in XSS for sem in row] +
                  [(sem, sem.n) for sem in XA] + [(sem, sem.n) for sem in XD])
        ln_stack.close()

        ffna_done = [(sem, sem.n) for sem in HDS]
        ffn_live.close()

        with ExitStack() as ph:
            qf = ph.enter_context(nc.sbuf_tensor("qf", [128, 8 * 1024], BF16))
            pt = ph.enter_context(nc.sbuf_tensor("pt", [128, 32 * 256], BF16))
            vbuf = ph.enter_context(nc.sbuf_tensor("vbuf", [128, 32 * 1024], BF16))
            kvf = ph.enter_context(nc.sbuf_tensor("kvf", [128, 8 * 4096], BF16))
            s_o = [ph.enter_context(nc.sbuf_tensor(f"s_o{i}", [128, 1024], F32)) for i in range(2)]
            pool = [ph.enter_context(nc.psum_tensor(f"pp{i}", [128, 512], F32)) for i in range(8)]

            for h in range(NH):
                first = (ffna_done + ln_mem) if h == 0 else []
                for dj in range(8):
                    P.emit("sync", lambda e, h=h, dj=dj: e.dma_start(
                        out=qf[:, dj * 1024:(dj + 1) * 1024].rearrange("p (a t) -> p a t", t=128),
                        in_=xnt_d[h * 1024 + dj * 128: h * 1024 + (dj + 1) * 128, :]
                            .rearrange("p (a f t) -> p a f t", f=4, t=128)[:, :, 3, :]),
                        reads=[("xnt", t, (h * 1024 + dj * 128) // 1024) for t in own],
                        writes=[("qf", dj)], extra_waits=first, inc=(QSD, 16),
                        dma=True, serialize=True)
                    P.emit("vector", lambda e, h=h, dj=dj: e.tensor_scalar(
                        qf[:, dj * 1024:(dj + 1) * 1024],
                        qf[:, dj * 1024:(dj + 1) * 1024],
                        qk_col[:, 8 * h + dj: 8 * h + dj + 1], None, ALU.mult),
                        reads=[("qf", dj), ("qk_col",)], writes=[("qf", dj)])
                for jq in range(4):
                    P.emit("gpsimd", lambda e, h=h, jq=jq: e.dma_start(
                        out=kvf[:, jq * 8192:(jq + 1) * 8192].rearrange("p (dj s2) -> p dj s2", dj=2),
                        in_=xnt_d[h * 1024 + jq * 256: h * 1024 + (jq + 1) * 256, :]
                            .rearrange("(dj p) s2 -> p dj s2", p=128)),
                        reads=[("xnt", t, h) for t in range(NROW)],
                        writes=[("kvf", 2 * jq), ("kvf", 2 * jq + 1)],
                        extra_waits=first, inc=(KVS[jq], 16), dma=True)
                for qd in range(4):
                    P.emit("sync", lambda e, h=h, qd=qd: e.dma_start(
                        out=vbuf[:, qd * 8192:(qd + 1) * 8192].rearrange("p (a d) -> p a d", d=1024),
                        in_=xna_d[qd * 1024:(qd + 1) * 1024, h * 1024:(h + 1) * 1024]
                            .rearrange("(a p) d -> p a d", p=128)),
                        reads=[("xna", 8 * qd + i) for i in range(8)],
                        writes=[("vbuf", 8 * qd + i) for i in range(8)],
                        extra_waits=first, inc=(VLS[qd], 16), dma=True)
                for p in range(4):
                    n_chunk = 2 * p + 2
                    for ch in range(n_chunk):
                        cset = 4 * (ch % 2)
                        for dj in range(8):
                            for sj in range(4):
                                P.emit("tensor", lambda e, ch=ch, cset=cset, sj=sj, dj=dj, p=p:
                                       e.matmul(
                                        pool[cset + sj][:, 0:256],
                                        kvf[:, dj * 4096 + ch * 512 + sj * 128:
                                              dj * 4096 + ch * 512 + (sj + 1) * 128],
                                        qf[:, dj * 1024 + 256 * p: dj * 1024 + 256 * (p + 1)],
                                        start=(dj == 0), stop=(dj == 7)),
                                        reads=[("kvf", dj), ("qf", dj)],
                                        writes=[("pool", cset + sj)],
                                        inc=((KVF, 1) if sj == 3 else None))
                        masked = (ch >= n_chunk - 2)
                        for sj in range(4):
                            stile = 4 * ch + sj
                            P.emit("scalar", lambda e, cset=cset, sj=sj, stile=stile: e.activation(
                                pt[:, stile * 256:(stile + 1) * 256],
                                pool[cset + sj][:, 0:256], AF.Sigmoid),
                                reads=[("pool", cset + sj)], writes=[("pt", stile)])
                            if masked:
                                r = stile - 8 * p
                                P.emit("vector", lambda e, stile=stile, r=r: e.tensor_mul(
                                    pt[:, stile * 256:(stile + 1) * 256],
                                    pt[:, stile * 256:(stile + 1) * 256],
                                    c_msk[:, r * 256:(r + 1) * 256]),
                                    reads=[("pt", stile), ("c_msk",)], writes=[("pt", stile)])
                    for jh in range(2):
                        n_st = 8 * p + 4 * (jh + 1)
                        j = 2 * p + jh
                        for stile in range(n_st):
                            for cb in range(2):
                                P.emit("tensor", lambda e, jh=jh, stile=stile, cb=cb, n_st=n_st: e.matmul(
                                    pool[2 * jh + cb][:],
                                    pt[:, stile * 256 + jh * 128: stile * 256 + jh * 128 + 128],
                                    vbuf[:, stile * 1024 + cb * 512: stile * 1024 + (cb + 1) * 512],
                                    start=(stile == 0), stop=(stile == n_st - 1)),
                                    reads=[("pt", stile), ("vbuf", stile)],
                                    writes=[("pool", 2 * jh + cb)],
                                    inc=((PVD, 1) if (stile == n_st - 1 and cb == 1) else None))
                        P.emit("vector", lambda e, jh=jh, h=h: e.tensor_mul(
                            s_o[jh][:, 0:512], pool[2 * jh][:],
                            v_bc[:, h * 1024:h * 1024 + 512]),
                            reads=[("pool", 2 * jh), ("v_bc", 2 * h)],
                            writes=[("s_o", jh, 0)])
                        P.emit("vector", lambda e, jh=jh, h=h: e.tensor_mul(
                            s_o[jh][:, 512:1024], pool[2 * jh + 1][:],
                            v_bc[:, h * 1024 + 512:(h + 1) * 1024]),
                            reads=[("pool", 2 * jh + 1), ("v_bc", 2 * h + 1)],
                            writes=[("s_o", jh, 1)], inc=(SEV, 1))
                        P.emit("sync", lambda e, j=j, h=h, jh=jh: e.dma_start(
                            out=s_d[j * 128:(j + 1) * 128, h * 1024:(h + 1) * 1024],
                            in_=s_o[jh][:]),
                            reads=[("s_o", jh, 0), ("s_o", jh, 1)],
                            writes=[("s_s", j, h)], inc=(SDS[jh], 16), dma=True)

        att_done = ([(PVD, PVD.n), (SEV, SEV.n)] +
                    [(sem, sem.n) for sem in SDS] + [(sem, sem.n) for sem in KVS])
        att_live.close()

        with ExitStack() as ph:
            h_sb = ph.enter_context(nc.sbuf_tensor("h_sb", [128, NFT * 1024], BF16))
            wd_b = [ph.enter_context(nc.sbuf_tensor(f"wd_b{i}", [128, 1024], BF16)) for i in range(3)]
            g_sb = [ph.enter_context(nc.sbuf_tensor(f"g_sb{i}", [128, 512], F32)) for i in range(3)]
            s_t = [ph.enter_context(nc.sbuf_tensor(f"s_t{i}", [128, 512], F32)) for i in range(3)]
            x_t = [ph.enter_context(nc.sbuf_tensor(f"x_t{i}", [128, 512], F32)) for i in range(3)]
            o_t = [ph.enter_context(nc.sbuf_tensor(f"o_t{i}", [128, 512], F32)) for i in range(3)]
            p_y = [ph.enter_context(nc.psum_tensor(f"p_y{i}", [128, 512], F32)) for i in range(8)]

            for ft in range(NFT):
                qeng = "sync" if ft % 2 == 0 else "scalar"
                qsem = HLD if ft % 2 == 0 else HLD2
                P.emit(qeng, lambda e, ft=ft: e.dma_start(
                    out=h_sb[:, ft * 1024:(ft + 1) * 1024], in_=h_d[ft]),
                    reads=[("h_s", ft)], writes=[("h_sb",)],
                    extra_waits=(att_done if ft < 2 else ()), inc=(qsem, 16), dma=True)

            for db in range(NDB):
                for ftp in range(NFT // 2):
                    gslot = ((NFT // 2) * db + ftp) % 3
                    P.emit("gpsimd", lambda e, db=db, ftp=ftp, gslot=gslot: e.dma_start(
                        out=wd_b[gslot][:].rearrange("p (l c) -> p l c", c=512),
                        in_=wd_d[db, 2 * ftp:2 * ftp + 2].rearrange("l p c -> p l c")),
                        writes=[("wd_b", gslot)],
                        extra_waits=(att_done if (db == 0 and ftp == 0) else ()),
                        inc=(WDS[gslot], 16), dma=True)
                    wpar = gslot
                    for fl in range(2):
                        ft = 2 * ftp + fl
                        for tt in range(8):
                            inc = None
                            if fl == 1 and tt == 7:
                                inc = (WDF, 1)
                            elif ft == NFT - 1:
                                inc = (YDT, 1)
                            P.emit("tensor", lambda e, par=wpar, fl=fl, ft=ft, tt=tt: e.matmul(
                                p_y[tt][:],
                                h_sb[:, ft * 1024 + tt * 128: ft * 1024 + (tt + 1) * 128],
                                wd_b[par][:, fl * 512:(fl + 1) * 512],
                                start=(ft == 0), stop=(ft == NFT - 1)),
                                reads=[("wd_b", wpar), ("h_sb",)],
                                writes=[("p_y", tt)], inc=inc)
                def sx_load(db, tt):
                    sl = tt % 3
                    P.emit("sync", lambda e, db=db, tt=tt, sl=sl: e.dma_start(
                        out=s_t[sl][:],
                        in_=s_d[tt * 128:(tt + 1) * 128, db * 512:(db + 1) * 512]),
                        reads=[("s_s", tt, db // 2)],
                        writes=[("s_t", sl)], inc=(STS[sl], 16), dma=True)
                    P.emit("sync", lambda e, db=db, tt=tt, sl=sl: e.dma_start(
                        out=x_t[sl][:],
                        in_=x_d.rearrange("(a p) d2 -> a p d2", p=128)[4 * tt + 3][:, db * 512:(db + 1) * 512]),
                        writes=[("x_t", sl)], inc=(XTS[sl], 16), dma=True)

                sx_load(db, 0)
                sx_load(db, 1)
                for tt in range(8):
                    sl = tt % 3
                    if tt + 2 < 8:
                        sx_load(db, tt + 2)
                    P.emit("scalar", lambda e, tt=tt, sl=sl: e.activation(
                        g_sb[sl][:], p_y[tt][:], AF.Silu),
                        reads=[("p_y", tt)], writes=[("g_sb", sl)])
                    P.emit("vector", lambda e, tt=tt, sl=sl: e.tensor_mul(
                        g_sb[sl][:], g_sb[sl][:], s_t[sl][:]),
                        reads=[("g_sb", sl), ("s_t", sl)],
                        writes=[("g_sb", sl)])
                    P.emit("vector", lambda e, tt=tt, sl=sl: e.tensor_add(
                        o_t[sl][:], g_sb[sl][:], x_t[sl][:]),
                        reads=[("g_sb", sl), ("x_t", sl)],
                        writes=[("o_t", sl)])
                    P.emit("scalar", lambda e, db=db, tt=tt, sl=sl: e.dma_start(
                        out=out_d[tt * 128:(tt + 1) * 128, db * 512:(db + 1) * 512],
                        in_=o_t[sl][:]),
                        reads=[("o_t", sl)], writes=[("outp", tt, db)],
                        inc=(ODS[sl], 16), dma=True)

            P.emit("sync", lambda e: e.nop(),
                   extra_waits=[(sem, sem.n) for sem in ODS])

        with nc.Block() as block:
            P.build(block)

    return nc


_NC_CACHE = []


def kernel(L0_attn_x, L0_attn_bv_q, L0_attn_bv_k, L0_attn_bv_v,
           L0_ffn_W_gate, L0_ffn_W_up, L0_ffn_W_down):
    x = np.ascontiguousarray(np.asarray(L0_attn_x, dtype=np.float32))
    bq = np.ascontiguousarray(np.asarray(L0_attn_bv_q, dtype=np.float32))
    bk = np.ascontiguousarray(np.asarray(L0_attn_bv_k, dtype=np.float32))
    bv = np.ascontiguousarray(np.asarray(L0_attn_bv_v, dtype=np.float32))
    wg = np.asarray(L0_ffn_W_gate, dtype=np.float32)
    wu = np.asarray(L0_ffn_W_up, dtype=np.float32)
    wd = np.asarray(L0_ffn_W_down, dtype=np.float32)

    wu_t = np.ascontiguousarray(wu.reshape(NFT, 128, 32, 128).transpose(0, 3, 2, 1))
    wg_t = np.ascontiguousarray(wg.reshape(NFT, 128, 32, 128).transpose(0, 3, 2, 1))
    wd_t = np.ascontiguousarray(wd.reshape(NDB, 512, NFT, 128).transpose(0, 2, 3, 1))
    idn = np.eye(128).astype(ml_dtypes.bfloat16)
    i32 = np.eye(32).astype(np.float32)

    in_maps = []
    for c in range(NCORE):
        b, k = c // 4, c % 4
        delta = [t for t in range(4) if t != k] + [k]
        order = []
        for g in range(8):
            order += [4 * g + d for d in delta]
        xp = np.ascontiguousarray(x[b].reshape(32, 128, D)[order].reshape(32 * 128, D))
        m = np.zeros((8, 128, 256), np.float32)
        pr = np.arange(128)[:, None]
        pc = np.arange(128)[None, :]
        for r in range(8):
            so = 4 * (r // 4) + delta[r % 4]
            for hh in range(2):
                m[r][:, hh * 128:(hh + 1) * 128] = (
                    128 * so + pr <= 128 * (4 * hh + k) + pc)
        in_maps.append({
            "x": xp, "wu": wu_t, "wg": wg_t, "wd": wd_t,
            "bq": bq, "bk": bk, "bv": bv,
            "msk": m.astype(ml_dtypes.bfloat16), "idn": idn, "i32": i32,
        })

    if not _NC_CACHE:
        _NC_CACHE.append(build_program())
    nc = _NC_CACHE[0]
    res = run_bass_kernel_spmd(nc, in_maps, core_ids=list(range(NCORE)))

    out = np.empty((2, 4096, D), np.float32)
    for c in range(NCORE):
        b, k = c // 4, c % 4
        shard = np.asarray(res.results[c]["out"]).reshape(NQT, 128, D)
        ob = out[b].reshape(32, 128, D)
        for j in range(NQT):
            ob[4 * j + k] = shard[j]
    return out


if __name__ == "__main__":
    nc = build_program()
    print("built instructions:", nc.n_instructions())


# version 11 (final)
# speedup vs baseline: 1.0981x; 1.0613x over previous
import numpy as np
import ml_dtypes
from contextlib import ExitStack

import concourse.bass as bass
import concourse.mybir as mybir
from concourse.bass_utils import run_bass_kernel_spmd

F32 = mybir.dt.float32
BF16 = mybir.dt.bfloat16
AF = mybir.ActivationFunctionType
ALU = mybir.AluOpType
AXX = mybir.AxisListType.X

D = 4096
DFF = 11008
NROW = 32
NQT = 8
NFT = 86
NDB = 8
NH = 4
NCORE = 8

ENGINES = ("sync", "scalar", "vector", "tensor", "gpsimd")


class Sem:
    def __init__(self, handle, name):
        self.h = handle
        self.name = name
        self.n = 0


class Prog:

    def __init__(self, chain_sems):
        self.records = []
        self.chain = chain_sems
        self.last_writer = {}
        self.readers = {}
        self.rec_engine = []
        self.rec_dma = []
        self.rec_inc = []
        self.eng_incs = {e: [] for e in ENGINES}

    def _cover(self, idx):
        e = self.rec_engine[idx]
        if self.rec_dma[idx]:
            inc = self.rec_inc[idx]
            assert inc is not None, f"DMA record {idx} has no completion inc"
            return (inc[0], inc[2])
        for (ridx, sem, val) in self.eng_incs[e]:
            if ridx >= idx:
                return (sem, val)
        raise RuntimeError(f"no cover for record {idx} on engine {e}")

    def emit(self, eng, fn, reads=(), writes=(), inc=None, extra_waits=(),
             dma=False, auto_inc=True, serialize=False):
        idx = len(self.records)
        waits = {}
        if serialize and inc is not None:
            extra_waits = list(extra_waits) + [(inc[0], inc[0].n)]

        def add_wait(sem, val):
            if val > 0 and waits.get(id(sem), (None, -1))[1] < val:
                waits[id(sem)] = (sem, val)

        for sem, val in extra_waits:
            add_wait(sem, val)

        def dep(pidx):
            if self.rec_engine[pidx] == "tensor" and eng == "tensor" \
                    and not self.rec_dma[pidx] and not dma:
                return
            sem, val = self._cover(pidx)
            add_wait(sem, val)

        for key in reads:
            w = self.last_writer.get(key)
            if w is not None:
                dep(w)
        for key in writes:
            for r in self.readers.get(key, ()):
                dep(r)
            w = self.last_writer.get(key)
            if w is not None:
                dep(w)

        for key in reads:
            self.readers.setdefault(key, []).append(idx)
        for key in writes:
            self.last_writer[key] = idx
            self.readers[key] = []

        if inc is None and auto_inc and not dma and eng in ("vector", "scalar", "gpsimd"):
            inc = (self.chain[eng], 1)
        rec_inc = None
        if inc is not None:
            sem, amt = inc
            sem.n += amt
            rec_inc = (sem, amt, sem.n)
            self.eng_incs[eng].append((idx, sem, sem.n))

        self.records.append((eng, fn, sorted(waits.values(), key=lambda t: id(t[0])), rec_inc))
        self.rec_engine.append(eng)
        self.rec_dma.append(dma)
        self.rec_inc.append(rec_inc)
        return idx

    def build(self, block):
        last_wait = {e: {} for e in ENGINES}
        stats = {e: [0, 0, 0] for e in ENGINES}

        def run_engine(name, eng):
            lw = last_wait[name]
            for (rec_eng, fn, waits, inc) in self.records:
                if rec_eng != name:
                    continue
                stats[name][0] += 1
                for sem, val in waits:
                    if lw.get(id(sem), -1) >= val:
                        stats[name][2] += 1
                        continue
                    eng.wait_ge(sem.h, val)
                    lw[id(sem)] = val
                    stats[name][1] += 1
                ins = fn(eng)
                if inc is not None:
                    ins.then_inc(inc[0].h, inc[1])

        @block.sync
        def _(e):
            run_engine("sync", e)

        @block.scalar
        def _(e):
            run_engine("scalar", e)

        @block.vector
        def _(e):
            run_engine("vector", e)

        @block.tensor
        def _(e):
            run_engine("tensor", e)

        @block.gpsimd
        def _(e):
            run_engine("gpsimd", e)

        return stats


def build_program():
    nc = bass.Bass("TRN2", target_bir_lowering=False, debug=False)

    x_d = nc.dram_tensor("x", [NROW * 128, D], F32, kind="ExternalInput").ap()
    wu_d = nc.dram_tensor("wu", [NFT, 128, D], F32, kind="ExternalInput").ap()
    wg_d = nc.dram_tensor("wg", [NFT, 128, D], F32, kind="ExternalInput").ap()
    wd_d = nc.dram_tensor("wd", [NDB, NFT, 128, 512], F32, kind="ExternalInput").ap()
    bq_d = nc.dram_tensor("bq", [4, 1024], F32, kind="ExternalInput").ap()
    bk_d = nc.dram_tensor("bk", [4, 1024], F32, kind="ExternalInput").ap()
    bv_d = nc.dram_tensor("bv", [4, 1024], F32, kind="ExternalInput").ap()
    msk_d = nc.dram_tensor("msk", [8, 128, 256], BF16, kind="ExternalInput").ap()
    idn_d = nc.dram_tensor("idn", [128, 128], BF16, kind="ExternalInput").ap()
    i32_d = nc.dram_tensor("i32", [32, 32], F32, kind="ExternalInput").ap()
    out_d = nc.dram_tensor("out", [1024, D], F32, kind="ExternalOutput").ap()

    xnt_d = nc.dram_tensor("xnt_s", [D, NROW * 128], BF16, kind="Internal").ap()
    xna_d = nc.dram_tensor("xna_s", [NROW * 128, D], BF16, kind="Internal").ap()
    h_d = nc.dram_tensor("h_s", [NFT, 128, 1024], BF16, kind="Internal").ap()
    s_d = nc.dram_tensor("s_s", [1024, D], F32, kind="Internal").ap()
    qkv_d = nc.dram_tensor("qkv_s", [4, 1024], F32, kind="Internal").ap()
    vvf_d = nc.dram_tensor("vvf_s", [4, 1024], F32, kind="Internal").ap()

    st = ExitStack()

    def mksem(name):
        return Sem(st.enter_context(nc.semaphore(name)), name)

    with st:
        VC, SC, GC = mksem("VC"), mksem("SC"), mksem("GC")
        TC = mksem("TC")
        INID = mksem("INID")
        XD = [mksem(f"XD{i}") for i in range(3)]
        XSS = [[mksem(f"XS{i}_{q}") for q in range(4)] for i in range(2)]
        XA = [mksem(f"XA{i}") for i in range(3)]
        XO = [mksem(f"XO{i}") for i in range(4)]
        WGS = [mksem(f"WG{i}") for i in range(2)]
        WUS = [mksem(f"WU{i}") for i in range(2)]
        GT = mksem("GT")
        UPS = mksem("UPS")
        HDS = [mksem(f"HD{i}") for i in range(2)]
        VLS = [mksem(f"VL{i}") for i in range(4)]
        QSD = mksem("QSD")
        KVS = [mksem(f"KV{i}") for i in range(4)]
        KVF = mksem("KVF")
        TPS = mksem("TPS")
        PVD = mksem("PVD")
        SEV = mksem("SEV")
        SDS = [mksem(f"SD{i}") for i in range(2)]
        HLD = mksem("HLD")
        HLD2 = mksem("HLD2")
        WDS = [mksem(f"WD{i}") for i in range(3)]
        WDF = mksem("WDF")
        YDT = mksem("YDT")
        ODS = [mksem(f"OD{i}") for i in range(3)]
        STS = [mksem(f"ST{i}") for i in range(3)]
        XTS = [mksem(f"XT{i}") for i in range(3)]

        chain = {"vector": VC, "scalar": SC, "gpsimd": GC}
        P = Prog(chain)

        c_cst = st.enter_context(nc.sbuf_tensor("c_cst", [128, 4], F32))
        c_idn = st.enter_context(nc.sbuf_tensor("c_idn", [128, 128], BF16))

        att_live = ExitStack()
        qk_col = att_live.enter_context(nc.sbuf_tensor("qk_col", [128, 32], F32))
        v_bc = att_live.enter_context(nc.sbuf_tensor("v_bc", [128, D], F32))
        c_msk = att_live.enter_context(nc.sbuf_tensor("c_msk", [128, 8 * 256], BF16))

        with ExitStack() as ph:
            s_bq = ph.enter_context(nc.sbuf_tensor("s_bq", [4, 1024], F32))
            s_bk = ph.enter_context(nc.sbuf_tensor("s_bk", [4, 1024], F32))
            s_bv = ph.enter_context(nc.sbuf_tensor("s_bv", [4, 1024], F32))
            s_al = ph.enter_context(nc.sbuf_tensor("s_al", [4, 4], F32))
            s_tmp = ph.enter_context(nc.sbuf_tensor("s_tmp", [4, 1024], F32))
            s_i32 = ph.enter_context(nc.sbuf_tensor("s_i32", [32, 32], F32))
            s_q32 = ph.enter_context(nc.sbuf_tensor("s_q32", [32, 128], F32))
            s_vf = ph.enter_context(nc.sbuf_tensor("s_vf", [1, D], F32))
            s_one = ph.enter_context(nc.sbuf_tensor("s_one", [1, 128], F32))
            p_qk = ph.enter_context(nc.psum_tensor("p_qk", [128, 32], F32))
            p_vb = ph.enter_context(nc.psum_tensor("p_vb", [128, 512], F32))

            bufs = {"s_bq": s_bq, "s_bk": s_bk, "s_bv": s_bv}
            for dst, src, key in [(c_idn, idn_d, "c_idn"), (s_i32, i32_d, "s_i32"),
                                  (s_bq, bq_d, "s_bq"), (s_bk, bk_d, "s_bk"),
                                  (s_bv, bv_d, "s_bv")]:
                P.emit("sync", lambda e, dst=dst, src=src: e.dma_start(
                    out=dst[:], in_=src), writes=[(key,)], inc=(INID, 16), dma=True, serialize=True)
            P.emit("sync", lambda e: e.dma_start(
                out=c_msk[:].rearrange("p (r t) -> p r t", r=8),
                in_=msk_d.rearrange("r p t -> p r t")),
                writes=[("c_msk",)], inc=(INID, 16), dma=True, serialize=True)
            P.emit("vector", lambda e: e.memset(c_cst[:, 0:1], 1e-5), writes=[("c_cst",)])
            P.emit("vector", lambda e: e.memset(c_cst[:, 1:2], 0.0), writes=[("c_cst",)])
            P.emit("vector", lambda e: e.memset(s_one[:, :], 1.0), writes=[("s_one",)])
            for col, key in ((0, "s_bq"), (1, "s_bk"), (2, "s_bv")):
                P.emit("vector", lambda e, s_w=bufs[key], col=col: e.reduce_sum(
                    s_al[:, col:col + 1], s_w[:], AXX, apply_absolute_value=True),
                    reads=[(key,)], writes=[("s_al",)])
            P.emit("scalar", lambda e: e.mul(s_al[:, 0:3], s_al[:, 0:3], 1.0 / 1024),
                   reads=[("s_al",)], writes=[("s_al",)])
            for key in ("s_bq", "s_bk", "s_bv"):
                P.emit("vector", lambda e, s_w=bufs[key]: e.tensor_scalar(
                    s_w[:], s_w[:], c_cst[0:4, 1:2], None, ALU.is_ge),
                    reads=[(key,), ("c_cst",)], writes=[(key,)])
                P.emit("vector", lambda e, s_w=bufs[key]: e.tensor_scalar(
                    s_w[:], s_w[:], 2.0, -1.0, ALU.mult, ALU.add),
                    reads=[(key,)], writes=[(key,)])
            P.emit("vector", lambda e: e.tensor_mul(s_tmp[:], s_bq[:], s_bk[:]),
                   reads=[("s_bq",), ("s_bk",)], writes=[("s_tmp",)])
            P.emit("vector", lambda e: e.tensor_scalar(
                s_tmp[:], s_tmp[:], s_al[:, 0:1], s_al[:, 1:2], ALU.mult, ALU.mult),
                reads=[("s_tmp",), ("s_al",)], writes=[("s_tmp",)])
            P.emit("vector", lambda e: e.tensor_scalar(
                s_tmp[:], s_tmp[:], 0.125, None, ALU.mult),
                reads=[("s_tmp",)], writes=[("s_tmp",)])
            P.emit("vector", lambda e: e.tensor_scalar(
                s_bv[:], s_bv[:], s_al[:, 2:3], None, ALU.mult),
                reads=[("s_bv",), ("s_al",)], writes=[("s_bv",)])
            P.emit("sync", lambda e: e.dma_start(out=qkv_d, in_=s_tmp[:]),
                   reads=[("s_tmp",)], writes=[("qkv_d",)], inc=(INID, 16), dma=True, serialize=True)
            P.emit("sync", lambda e: e.dma_start(out=vvf_d, in_=s_bv[:]),
                   reads=[("s_bv",)], writes=[("vvf_d",)], inc=(INID, 16), dma=True, serialize=True)
            P.emit("sync", lambda e: e.dma_start(
                out=s_q32[:],
                in_=qkv_d.rearrange("a b -> (a b)").rearrange("(dt p) -> dt p", p=128)),
                reads=[("qkv_d",)], writes=[("s_q32",)], inc=(INID, 16), dma=True, serialize=True)
            P.emit("tensor", lambda e: e.matmul(
                p_qk[:], s_q32[:], s_i32[:], start=True, stop=True),
                reads=[("s_q32",), ("s_i32",)], writes=[("p_qk",)], inc=(TC, 1))
            P.emit("vector", lambda e: e.tensor_copy(qk_col[:], p_qk[:]),
                   reads=[("p_qk",)], writes=[("qk_col",)])
            P.emit("sync", lambda e: e.dma_start(
                out=s_vf[:],
                in_=vvf_d.rearrange("a b -> (a b)").rearrange("(o f) -> o f", o=1)),
                reads=[("vvf_d",)], writes=[("s_vf",)], inc=(INID, 16), dma=True, serialize=True)
            for cb in range(8):
                P.emit("tensor", lambda e, cb=cb: e.matmul(
                    p_vb[:], s_one[0:1, :], s_vf[0:1, cb * 512:(cb + 1) * 512],
                    start=True, stop=True),
                    reads=[("s_one",), ("s_vf",)], writes=[("p_vb",)], inc=(TC, 1))
                P.emit("vector", lambda e, cb=cb: e.tensor_copy(
                    v_bc[:, cb * 512:(cb + 1) * 512], p_vb[:]),
                    reads=[("p_vb",)], writes=[("v_bc", cb)])

        init_done = [(INID, INID.n), (VC, VC.n), (SC, SC.n)]

        own = [4 * j + 3 for j in range(NQT)]
        rest = [i for i in range(NROW) if i not in own]
        order = own + rest

        ffn_live = ExitStack()
        xn_own = ffn_live.enter_context(nc.sbuf_tensor("xn_own", [128, 32 * 1024], BF16))
        wu_b = [ffn_live.enter_context(nc.sbuf_tensor(f"wu_b{i}", [128, D], BF16)) for i in range(2)]
        wg_b = [ffn_live.enter_context(nc.sbuf_tensor(f"wg_b{i}", [128, D], BF16)) for i in range(2)]
        sg_t = [ffn_live.enter_context(nc.sbuf_tensor(f"sg_t{i}", [128, 1024], F32)) for i in range(2)]
        h_b = [ffn_live.enter_context(nc.sbuf_tensor(f"h_b{i}", [128, 1024], BF16)) for i in range(2)]
        p_u = [ffn_live.enter_context(nc.psum_tensor(f"p_u{i}", [128, 512], F32)) for i in range(2)]
        p_g = [ffn_live.enter_context(nc.psum_tensor(f"p_g{i}", [128, 512], F32)) for i in range(2)]

        ln_stack = ExitStack()
        x_b = [ln_stack.enter_context(nc.sbuf_tensor(f"x_b{i}", [128, D], F32)) for i in range(2)]
        xn_b = [ln_stack.enter_context(nc.sbuf_tensor(f"xn_b{i}", [128, D], BF16)) for i in range(2)]
        xt_b = [ln_stack.enter_context(nc.sbuf_tensor(f"xt_b{i}", [128, D], BF16)) for i in range(2)]
        dump = xn_b[0]
        st2 = ln_stack.enter_context(nc.sbuf_tensor("st2", [128, 256], F32))
        p_tr = [ln_stack.enter_context(nc.psum_tensor(f"p_tr{i}", [128, 512], F32)) for i in range(2)]

        def pass_load(n, ti, gate=False):
            par = n % 2
            xb = x_b[par]
            P.emit("sync", lambda e, ti=ti, xb=xb: e.dma_start(
                out=xb[:], in_=x_d[ti * 128:(ti + 1) * 128, :]),
                writes=[("x_b", par)], extra_waits=(init_done if gate else ()),
                inc=(XD[par], 16), dma=True)

        def pass1_stats(n, ti):
            par = n % 2
            xb = x_b[par]
            xk = ("x_b", par)
            P.emit("vector", lambda e, xb=xb, n=n: e.reduce_sum(
                st2[:, n:n + 1], xb[:], AXX), reads=[xk], writes=[("st2s", n)])
            P.emit("scalar", lambda e, xb=xb, n=n: e.activation(
                dump[:], xb[:], AF.Square, accum_out=st2[:, 32 + n:33 + n]),
                reads=[xk], writes=[("xn_b", 0), ("st2q", n)])

        def emit_chain(c0, c1):
            sq = ([("st2s", t) for t in range(c0, c1)] +
                  [("st2q", t) for t in range(c0, c1)])
            ck = ("st2c", c0)
            P.emit("vector", lambda e, c0=c0, c1=c1: e.tensor_scalar(
                st2[:, 64 + c0:64 + c1], st2[:, c0:c1], -1.0 / D, None, ALU.mult),
                reads=sq, writes=[ck])
            P.emit("vector", lambda e, c0=c0, c1=c1: e.tensor_scalar(
                st2[:, 96 + c0:96 + c1], st2[:, 32 + c0:32 + c1], 1.0 / D, None, ALU.mult),
                reads=[ck], writes=[ck])
            P.emit("vector", lambda e, c0=c0, c1=c1: e.tensor_mul(
                st2[:, 128 + c0:128 + c1], st2[:, 64 + c0:64 + c1], st2[:, 64 + c0:64 + c1]),
                reads=[ck], writes=[ck])
            P.emit("vector", lambda e, c0=c0, c1=c1: e.tensor_sub(
                st2[:, 160 + c0:160 + c1], st2[:, 96 + c0:96 + c1], st2[:, 128 + c0:128 + c1]),
                reads=[ck], writes=[ck])
            P.emit("scalar", lambda e, c0=c0, c1=c1: e.activation(
                st2[:, 96 + c0:96 + c1], st2[:, 160 + c0:160 + c1], AF.Sqrt, bias=c_cst[:, 0:1]),
                reads=[ck, ("c_cst",)], writes=[ck])
            P.emit("vector", lambda e, c0=c0, c1=c1: e.reciprocal(
                st2[:, 192 + c0:192 + c1], st2[:, 96 + c0:96 + c1]),
                reads=[ck], writes=[ck])
            P.emit("vector", lambda e, c0=c0, c1=c1: e.tensor_mul(
                st2[:, 224 + c0:224 + c1], st2[:, 64 + c0:64 + c1], st2[:, 192 + c0:192 + c1]),
                reads=[ck], writes=[ck])

        def pass2_norm(n, ti):
            par = n % 2
            xb, xnb = x_b[par], xn_b[par]
            xk, xnk = ("x_b", par), ("xn_b", par)
            ck = ("st2c", 0 if n < 8 else 8)
            P.emit("scalar", lambda e, xb=xb, xnb=xnb, n=n: e.activation(
                xnb[:], xb[:], AF.Identity,
                bias=st2[:, 224 + n:225 + n], scale=st2[:, 192 + n:193 + n]),
                reads=[xk, ck], writes=[xnk])
            P.emit("sync", lambda e, ti=ti, xnb=xnb: e.dma_start(
                out=xna_d[ti * 128:(ti + 1) * 128, :], in_=xnb[:]),
                reads=[xnk], writes=[("xna", ti)], inc=(XA[par], 16), dma=True)

        def pass2_b(n, ti):
            par = n % 2
            tpr = n % 2
            xnb, xtb = xn_b[par], xt_b[tpr]
            xnk, xtk = ("xn_b", par), ("xt_b", tpr)
            for g in range(8):
                tpar = g % 2
                for u in range(4):
                    dj = 4 * g + u
                    P.emit("tensor", lambda e, xnb=xnb, dj=dj, u=u, tpar=tpar: e.matmul(
                        p_tr[tpar][:, u * 128:(u + 1) * 128],
                        xnb[:, dj * 128:(dj + 1) * 128], c_idn[:],
                        start=True, stop=True),
                        reads=[xnk, ("c_idn",)], writes=[("p_tr", tpar)],
                        inc=((TPS, 1) if u == 3 else None))
                P.emit("vector", lambda e, xtb=xtb, g=g, tpar=tpar: e.tensor_copy(
                    xtb[:, g * 512:(g + 1) * 512], p_tr[tpar][:]),
                    reads=[("p_tr", tpar)], writes=[xtk])
                if g % 2 == 1:
                    q = g // 2
                    P.emit("sync", lambda e, ti=ti, xtb=xtb, q=q: e.dma_start(
                        out=xnt_d[q * 1024:(q + 1) * 1024, ti * 128:(ti + 1) * 128]
                            .rearrange("(j p) t -> p j t", p=128),
                        in_=xtb[:, q * 1024:(q + 1) * 1024]
                            .rearrange("p (j t) -> p j t", t=128)),
                        reads=[xtk], writes=[("xnt", ti, q)],
                        inc=(XSS[tpr][q], 16), dma=True)

        for n in range(NQT):
            pass_load(n, order[n], gate=(n < 2))
            pass1_stats(n, order[n])
        emit_chain(0, NQT)
        for n in range(NQT):
            pass_load(n, order[n])
            pass2_norm(n, order[n])
            pass2_b(n, order[n])

        for dj in range(32):
            P.emit("sync", lambda e, dj=dj: e.dma_start(
                out=xn_own[:, dj * 1024:(dj + 1) * 1024].rearrange("p (j t) -> p j t", t=128),
                in_=xnt_d[dj * 128:(dj + 1) * 128, :]
                    .rearrange("p (a f t) -> p a f t", f=4, t=128)[:, :, 3, :]),
                reads=[("xnt", t, dj // 8) for t in own], writes=[("xn_own", dj)],
                inc=(XO[dj % 4], 16), dma=True)

        if True:
            def ffn_a_ft(ft):
                par = ft % 2
                P.emit("gpsimd", lambda e, ft=ft, par=par: e.dma_start(
                    out=wg_b[par][:], in_=wg_d[ft]),
                    writes=[("wg_b", par)], extra_waits=(init_done if ft == 0 else ()),
                    inc=(WGS[par], 16), dma=True)
                P.emit("gpsimd", lambda e, ft=ft, par=par: e.dma_start(
                    out=wu_b[par][:], in_=wu_d[ft]),
                    writes=[("wu_b", par)], inc=(WUS[par], 16), dma=True)
                xo_w = [(sem, sem.n) for sem in XO] if ft == 0 else []
                for half in range(2):
                    hp = (2 * ft + half) % 2
                    for dj in range(32):
                        P.emit("tensor", lambda e, hp=hp, par=par, half=half, dj=dj: e.matmul(
                            p_g[hp][:],
                            wg_b[par][:, dj * 128:(dj + 1) * 128],
                            xn_own[:, dj * 1024 + half * 512: dj * 1024 + (half + 1) * 512],
                            start=(dj == 0), stop=(dj == 31)),
                            reads=[("wg_b", par), ("xn_own", dj)],
                            writes=[("p_g", hp)], extra_waits=(xo_w if (half == 0 and dj == 0) else []),
                            inc=((GT, 1) if dj == 31 else None))
                    for dj in range(32):
                        P.emit("tensor", lambda e, hp=hp, par=par, half=half, dj=dj: e.matmul(
                            p_u[hp][:],
                            wu_b[par][:, dj * 128:(dj + 1) * 128],
                            xn_own[:, dj * 1024 + half * 512: dj * 1024 + (half + 1) * 512],
                            start=(dj == 0), stop=(dj == 31)),
                            reads=[("wu_b", par), ("xn_own", dj)],
                            writes=[("p_u", hp)],
                            inc=((UPS, 1) if dj == 31 else None))
                    P.emit("scalar", lambda e, hp=hp, par=par, half=half: e.activation(
                        sg_t[par][:, half * 512:(half + 1) * 512], p_g[hp][:], AF.Silu),
                        reads=[("p_g", hp)], writes=[("sg_t", par, half)])
                    P.emit("vector", lambda e, hp=hp, par=par, half=half: e.tensor_mul(
                        h_b[par][:, half * 512:(half + 1) * 512],
                        sg_t[par][:, half * 512:(half + 1) * 512], p_u[hp][:]),
                        reads=[("sg_t", par, half), ("p_u", hp)], writes=[("h_b", par, half)])
                P.emit("sync", lambda e, ft=ft, par=par: e.dma_start(
                    out=h_d[ft], in_=h_b[par][:]),
                    reads=[("h_b", par, 0), ("h_b", par, 1)], writes=[("h_s", ft)],
                    inc=(HDS[par], 16), dma=True)

            NRST = NROW - NQT
            for ft in range(NFT):
                ffn_a_ft(ft)
                if 28 <= ft <= 27 + NRST:
                    pass2_b(NQT + ft - 28, order[NQT + ft - 28])
                if 26 <= ft <= 25 + NRST:
                    pass2_norm(NQT + ft - 26, order[NQT + ft - 26])
                if 1 <= ft <= NRST:
                    pass1_stats(NQT + ft - 1, order[NQT + ft - 1])
                if ft == NRST + 1:
                    emit_chain(NQT, NROW)
                if ft <= NRST - 1:
                    pass_load(NQT + ft, order[NQT + ft])
                if 25 <= ft <= 24 + NRST:
                    pass_load(NQT + ft - 25, order[NQT + ft - 25])

        ln_mem = ([(sem, sem.n) for row in XSS for sem in row] +
                  [(sem, sem.n) for sem in XA] + [(sem, sem.n) for sem in XD])
        ln_stack.close()

        ffna_done = [(sem, sem.n) for sem in HDS]
        ffn_live.close()

        with ExitStack() as ph:
            qf = ph.enter_context(nc.sbuf_tensor("qf", [128, 8 * 1024], BF16))
            pt = ph.enter_context(nc.sbuf_tensor("pt", [128, 32 * 256], BF16))
            vbuf = ph.enter_context(nc.sbuf_tensor("vbuf", [128, 32 * 1024], BF16))
            kvf = ph.enter_context(nc.sbuf_tensor("kvf", [128, 8 * 4096], BF16))
            s_o = [ph.enter_context(nc.sbuf_tensor(f"s_o{i}", [128, 1024], F32)) for i in range(2)]
            pool = [ph.enter_context(nc.psum_tensor(f"pp{i}", [128, 512], F32)) for i in range(8)]

            for h in range(NH):
                first = (ffna_done + ln_mem) if h == 0 else []
                for dj in range(8):
                    P.emit("sync", lambda e, h=h, dj=dj: e.dma_start(
                        out=qf[:, dj * 1024:(dj + 1) * 1024].rearrange("p (a t) -> p a t", t=128),
                        in_=xnt_d[h * 1024 + dj * 128: h * 1024 + (dj + 1) * 128, :]
                            .rearrange("p (a f t) -> p a f t", f=4, t=128)[:, :, 3, :]),
                        reads=[("xnt", t, (h * 1024 + dj * 128) // 1024) for t in own],
                        writes=[("qf", dj)], extra_waits=first, inc=(QSD, 16),
                        dma=True, serialize=True)
                    P.emit("vector", lambda e, h=h, dj=dj: e.tensor_scalar(
                        qf[:, dj * 1024:(dj + 1) * 1024],
                        qf[:, dj * 1024:(dj + 1) * 1024],
                        qk_col[:, 8 * h + dj: 8 * h + dj + 1], None, ALU.mult),
                        reads=[("qf", dj), ("qk_col",)], writes=[("qf", dj)])
                for jq in range(4):
                    P.emit("gpsimd", lambda e, h=h, jq=jq: e.dma_start(
                        out=kvf[:, jq * 8192:(jq + 1) * 8192].rearrange("p (dj s2) -> p dj s2", dj=2),
                        in_=xnt_d[h * 1024 + jq * 256: h * 1024 + (jq + 1) * 256, :]
                            .rearrange("(dj p) s2 -> p dj s2", p=128)),
                        reads=[("xnt", t, h) for t in range(NROW)],
                        writes=[("kvf", 2 * jq), ("kvf", 2 * jq + 1)],
                        extra_waits=first, inc=(KVS[jq], 16), dma=True)
                for qd in range(4):
                    P.emit("sync", lambda e, h=h, qd=qd: e.dma_start(
                        out=vbuf[:, qd * 8192:(qd + 1) * 8192].rearrange("p (a d) -> p a d", d=1024),
                        in_=xna_d[qd * 1024:(qd + 1) * 1024, h * 1024:(h + 1) * 1024]
                            .rearrange("(a p) d -> p a d", p=128)),
                        reads=[("xna", 8 * qd + i) for i in range(8)],
                        writes=[("vbuf", 8 * qd + i) for i in range(8)],
                        extra_waits=first, inc=(VLS[qd], 16), dma=True)
                for p in range(4):
                    n_chunk = 2 * p + 2
                    for ch in range(n_chunk):
                        cset = 4 * (ch % 2)
                        for dj in range(8):
                            for sj in range(4):
                                P.emit("tensor", lambda e, ch=ch, cset=cset, sj=sj, dj=dj, p=p:
                                       e.matmul(
                                        pool[cset + sj][:, 0:256],
                                        kvf[:, dj * 4096 + ch * 512 + sj * 128:
                                              dj * 4096 + ch * 512 + (sj + 1) * 128],
                                        qf[:, dj * 1024 + 256 * p: dj * 1024 + 256 * (p + 1)],
                                        start=(dj == 0), stop=(dj == 7)),
                                        reads=[("kvf", dj), ("qf", dj)],
                                        writes=[("pool", cset + sj)],
                                        inc=((KVF, 1) if sj == 3 else None))
                        masked = (ch >= n_chunk - 2)
                        for sj in range(4):
                            stile = 4 * ch + sj
                            P.emit("scalar", lambda e, cset=cset, sj=sj, stile=stile: e.activation(
                                pt[:, stile * 256:(stile + 1) * 256],
                                pool[cset + sj][:, 0:256], AF.Sigmoid),
                                reads=[("pool", cset + sj)], writes=[("pt", stile)])
                            if masked:
                                r = stile - 8 * p
                                P.emit("vector", lambda e, stile=stile, r=r: e.tensor_mul(
                                    pt[:, stile * 256:(stile + 1) * 256],
                                    pt[:, stile * 256:(stile + 1) * 256],
                                    c_msk[:, r * 256:(r + 1) * 256]),
                                    reads=[("pt", stile), ("c_msk",)], writes=[("pt", stile)])
                    for jh in range(2):
                        n_st = 8 * p + 4 * (jh + 1)
                        j = 2 * p + jh
                        for stile in range(n_st):
                            for cb in range(2):
                                P.emit("tensor", lambda e, jh=jh, stile=stile, cb=cb, n_st=n_st: e.matmul(
                                    pool[2 * jh + cb][:],
                                    pt[:, stile * 256 + jh * 128: stile * 256 + jh * 128 + 128],
                                    vbuf[:, stile * 1024 + cb * 512: stile * 1024 + (cb + 1) * 512],
                                    start=(stile == 0), stop=(stile == n_st - 1)),
                                    reads=[("pt", stile), ("vbuf", stile)],
                                    writes=[("pool", 2 * jh + cb)],
                                    inc=((PVD, 1) if (stile == n_st - 1 and cb == 1) else None))
                        P.emit("vector", lambda e, jh=jh, h=h: e.tensor_mul(
                            s_o[jh][:, 0:512], pool[2 * jh][:],
                            v_bc[:, h * 1024:h * 1024 + 512]),
                            reads=[("pool", 2 * jh), ("v_bc", 2 * h)],
                            writes=[("s_o", jh, 0)])
                        P.emit("vector", lambda e, jh=jh, h=h: e.tensor_mul(
                            s_o[jh][:, 512:1024], pool[2 * jh + 1][:],
                            v_bc[:, h * 1024 + 512:(h + 1) * 1024]),
                            reads=[("pool", 2 * jh + 1), ("v_bc", 2 * h + 1)],
                            writes=[("s_o", jh, 1)], inc=(SEV, 1))
                        P.emit("sync", lambda e, j=j, h=h, jh=jh: e.dma_start(
                            out=s_d[j * 128:(j + 1) * 128, h * 1024:(h + 1) * 1024],
                            in_=s_o[jh][:]),
                            reads=[("s_o", jh, 0), ("s_o", jh, 1)],
                            writes=[("s_s", j, h)], inc=(SDS[jh], 16), dma=True)

        att_done = ([(PVD, PVD.n), (SEV, SEV.n)] +
                    [(sem, sem.n) for sem in SDS] + [(sem, sem.n) for sem in KVS])
        att_live.close()

        with ExitStack() as ph:
            h_sb = ph.enter_context(nc.sbuf_tensor("h_sb", [128, NFT * 1024], BF16))
            wd_b = [ph.enter_context(nc.sbuf_tensor(f"wd_b{i}", [128, 1024], BF16)) for i in range(3)]
            g_sb = [ph.enter_context(nc.sbuf_tensor(f"g_sb{i}", [128, 512], F32)) for i in range(3)]
            s_t = [ph.enter_context(nc.sbuf_tensor(f"s_t{i}", [128, 512], F32)) for i in range(3)]
            x_t = [ph.enter_context(nc.sbuf_tensor(f"x_t{i}", [128, 512], F32)) for i in range(3)]
            o_t = [ph.enter_context(nc.sbuf_tensor(f"o_t{i}", [128, 512], F32)) for i in range(3)]
            p_y = [ph.enter_context(nc.psum_tensor(f"p_y{i}", [128, 512], F32)) for i in range(8)]

            ch_bounds = [(c * 8, min((c + 1) * 8, NFT)) for c in range(11)]
            for c, (f0, f1) in enumerate(ch_bounds):
                qeng = "sync" if c % 2 == 0 else "scalar"
                qsem = HLD if c % 2 == 0 else HLD2
                P.emit(qeng, lambda e, f0=f0, f1=f1: e.dma_start(
                    out=h_sb[:, f0 * 1024:f1 * 1024].rearrange("p (a c2) -> p a c2", c2=1024),
                    in_=h_d[f0:f1].rearrange("a p c2 -> p a c2")),
                    reads=[("h_s", ft) for ft in range(f0, f1)], writes=[("h_sb", c)],
                    extra_waits=(att_done if c < 2 else ()), inc=(qsem, 16),
                    dma=True, serialize=True)

            for db in range(NDB):
                for ftp in range(NFT // 2):
                    gslot = ((NFT // 2) * db + ftp) % 3
                    P.emit("gpsimd", lambda e, db=db, ftp=ftp, gslot=gslot: e.dma_start(
                        out=wd_b[gslot][:].rearrange("p (l c) -> p l c", c=512),
                        in_=wd_d[db, 2 * ftp:2 * ftp + 2].rearrange("l p c -> p l c")),
                        writes=[("wd_b", gslot)],
                        extra_waits=(att_done if (db == 0 and ftp == 0) else ()),
                        inc=(WDS[gslot], 16), dma=True)
                    wpar = gslot
                    for fl in range(2):
                        ft = 2 * ftp + fl
                        for tt in range(8):
                            inc = None
                            if fl == 1 and tt == 7:
                                inc = (WDF, 1)
                            elif ft == NFT - 1:
                                inc = (YDT, 1)
                            P.emit("tensor", lambda e, par=wpar, fl=fl, ft=ft, tt=tt: e.matmul(
                                p_y[tt][:],
                                h_sb[:, ft * 1024 + tt * 128: ft * 1024 + (tt + 1) * 128],
                                wd_b[par][:, fl * 512:(fl + 1) * 512],
                                start=(ft == 0), stop=(ft == NFT - 1)),
                                reads=[("wd_b", wpar), ("h_sb", ft // 8)],
                                writes=[("p_y", tt)], inc=inc)
                def sx_load(db, tt):
                    sl = tt % 3
                    P.emit("sync", lambda e, db=db, tt=tt, sl=sl: e.dma_start(
                        out=s_t[sl][:],
                        in_=s_d[tt * 128:(tt + 1) * 128, db * 512:(db + 1) * 512]),
                        reads=[("s_s", tt, db // 2)],
                        writes=[("s_t", sl)], inc=(STS[sl], 16), dma=True)
                    P.emit("sync", lambda e, db=db, tt=tt, sl=sl: e.dma_start(
                        out=x_t[sl][:],
                        in_=x_d.rearrange("(a p) d2 -> a p d2", p=128)[4 * tt + 3][:, db * 512:(db + 1) * 512]),
                        writes=[("x_t", sl)], inc=(XTS[sl], 16), dma=True)

                sx_load(db, 0)
                sx_load(db, 1)
                for tt in range(8):
                    sl = tt % 3
                    if tt + 2 < 8:
                        sx_load(db, tt + 2)
                    P.emit("scalar", lambda e, tt=tt, sl=sl: e.activation(
                        g_sb[sl][:], p_y[tt][:], AF.Silu),
                        reads=[("p_y", tt)], writes=[("g_sb", sl)])
                    P.emit("vector", lambda e, tt=tt, sl=sl: e.tensor_mul(
                        g_sb[sl][:], g_sb[sl][:], s_t[sl][:]),
                        reads=[("g_sb", sl), ("s_t", sl)],
                        writes=[("g_sb", sl)])
                    P.emit("vector", lambda e, tt=tt, sl=sl: e.tensor_add(
                        o_t[sl][:], g_sb[sl][:], x_t[sl][:]),
                        reads=[("g_sb", sl), ("x_t", sl)],
                        writes=[("o_t", sl)])
                    P.emit("scalar", lambda e, db=db, tt=tt, sl=sl: e.dma_start(
                        out=out_d[tt * 128:(tt + 1) * 128, db * 512:(db + 1) * 512],
                        in_=o_t[sl][:]),
                        reads=[("o_t", sl)], writes=[("outp", tt, db)],
                        inc=(ODS[sl], 16), dma=True)

            P.emit("sync", lambda e: e.nop(),
                   extra_waits=[(sem, sem.n) for sem in ODS])

        with nc.Block() as block:
            P.build(block)

    return nc


_NC_CACHE = []


def kernel(L0_attn_x, L0_attn_bv_q, L0_attn_bv_k, L0_attn_bv_v,
           L0_ffn_W_gate, L0_ffn_W_up, L0_ffn_W_down):
    x = np.ascontiguousarray(np.asarray(L0_attn_x, dtype=np.float32))
    bq = np.ascontiguousarray(np.asarray(L0_attn_bv_q, dtype=np.float32))
    bk = np.ascontiguousarray(np.asarray(L0_attn_bv_k, dtype=np.float32))
    bv = np.ascontiguousarray(np.asarray(L0_attn_bv_v, dtype=np.float32))
    wg = np.asarray(L0_ffn_W_gate, dtype=np.float32)
    wu = np.asarray(L0_ffn_W_up, dtype=np.float32)
    wd = np.asarray(L0_ffn_W_down, dtype=np.float32)

    wu_t = np.ascontiguousarray(wu.reshape(NFT, 128, 32, 128).transpose(0, 3, 2, 1))
    wg_t = np.ascontiguousarray(wg.reshape(NFT, 128, 32, 128).transpose(0, 3, 2, 1))
    wd_t = np.ascontiguousarray(wd.reshape(NDB, 512, NFT, 128).transpose(0, 2, 3, 1))
    idn = np.eye(128).astype(ml_dtypes.bfloat16)
    i32 = np.eye(32).astype(np.float32)

    in_maps = []
    for c in range(NCORE):
        b, k = c // 4, c % 4
        delta = [t for t in range(4) if t != k] + [k]
        order = []
        for g in range(8):
            order += [4 * g + d for d in delta]
        xp = np.ascontiguousarray(x[b].reshape(32, 128, D)[order].reshape(32 * 128, D))
        m = np.zeros((8, 128, 256), np.float32)
        pr = np.arange(128)[:, None]
        pc = np.arange(128)[None, :]
        for r in range(8):
            so = 4 * (r // 4) + delta[r % 4]
            for hh in range(2):
                m[r][:, hh * 128:(hh + 1) * 128] = (
                    128 * so + pr <= 128 * (4 * hh + k) + pc)
        in_maps.append({
            "x": xp, "wu": wu_t, "wg": wg_t, "wd": wd_t,
            "bq": bq, "bk": bk, "bv": bv,
            "msk": m.astype(ml_dtypes.bfloat16), "idn": idn, "i32": i32,
        })

    if not _NC_CACHE:
        _NC_CACHE.append(build_program())
    nc = _NC_CACHE[0]
    res = run_bass_kernel_spmd(nc, in_maps, core_ids=list(range(NCORE)))

    out = np.empty((2, 4096, D), np.float32)
    for c in range(NCORE):
        b, k = c // 4, c % 4
        shard = np.asarray(res.results[c]["out"]).reshape(NQT, 128, D)
        ob = out[b].reshape(32, 128, D)
        for j in range(NQT):
            ob[4 * j + k] = shard[j]
    return out


if __name__ == "__main__":
    nc = build_program()
    print("built instructions:", nc.n_instructions())
